# version 10
# baseline (speedup 1.0000x reference)
import numpy as np
import ml_dtypes
from contextlib import ExitStack

import concourse.bass as bass
import concourse.mybir as mybir
from concourse.bass_utils import run_bass_kernel_spmd
from concourse.library_overlay import lower_extended_insts

F32 = mybir.dt.float32
F16 = mybir.dt.float16
BF16 = mybir.dt.bfloat16
F8 = mybir.dt.float8e4
U8 = mybir.dt.uint8
AL = mybir.AluOpType
AF = mybir.ActivationFunctionType

D = 1024
H = 8
HD = 128
T = 2048
NCORE = 8
SEQ_PER_CORE = 2
SUB = 512
NSUB = T // SUB
CH = 128
NCH_SUB = SUB // CH


_OPS = {}


def _f(x):
    return np.float64 if np.asarray(x).dtype == np.float64 else np.float32


def _col(v, f):
    v = np.asarray(v, f)
    return v.reshape(-1, 1) if v.ndim else v


def _ref_qpos(in0, in1, s0, s1, imm2):
    f = _f(in0)
    z = np.asarray(in0, f)
    c0 = _col(s0, f)
    c1 = _col(s1, f)
    a = (z * c0).astype(f)
    b = (a - z).astype(f)
    v = (a - b).astype(f)
    t = (z + c1).astype(f)
    s = (t - c1).astype(f)
    return np.where(z >= f(imm2), v, s)


def _ref_qsel(in0, in1, s0, s1, imm2):
    f = _f(in0)
    z = np.asarray(in0, f)
    c1 = _col(s1, f)
    t = (z + c1).astype(f)
    s = (t - c1).astype(f)
    q = (z * z).astype(f)
    return np.where(q >= f(imm2), np.asarray(in1, f), s)


def _ref_lerp(in0, in1, s0, s1, imm2):
    f = _f(in0)
    a = np.asarray(in0, f)
    d = (np.asarray(in1, f) - a).astype(f)
    m = (d * _col(s0, f)).astype(f)
    return (a + m).astype(f)


def _ref_nlerp(in0, in1, s0, s1, imm2):
    f = _f(in0)
    t = (np.asarray(in1, f) * _col(s0, f)).astype(f)
    u = (f(1.0) - t).astype(f)
    return (np.asarray(in0, f) * u).astype(f)


def _ref_td4s(in0, in1, s0, s1, imm2):
    f = _f(in0)
    s = (np.asarray(in0, f) ** 2).astype(f)
    a = (s >= _col(s0, f)).astype(f)
    b = (s >= _col(s1, f)).astype(f)
    c = (s >= f(imm2)).astype(f)
    return ((a - b) - b) + c


def _ref_td4w(in0, in1, s0, s1, imm2):
    f = _f(in0)
    s = (np.asarray(in0, f) ** 2).astype(f)
    a = (s >= _col(s0, f)).astype(f)
    b = (s >= _col(s1, f)).astype(f)
    return a - b


def _register_ops():
    if _OPS:
        return _OPS
    from concourse.dve_op import DveOp
    from concourse.dve_spec import Spec, Src0, Src1, C0, C1, C2, One, select, sq, lower
    from concourse import dve_ops as dops

    a = Src0 * C0
    b = a - Src0
    v = a - b
    t = Src0 + C1
    s = t - C1
    spec_qpos = Spec(body=select(Src0 >= C2, v, s))
    spec_qsel = Spec(body=select(sq(Src0) >= C2, Src1, s))
    dd = Src1 - Src0
    spec_lerp = Spec(body=Src0 + dd * C0)
    tt = Src1 * C0
    spec_nlerp = Spec(body=Src0 * (One - tt))
    sx = sq(Src0)
    ca = (sx >= C0)
    cb = (sx >= C1)
    cc = (sx >= C2)
    spec_td4s = Spec(body=((ca - cb) - cb) + cc)
    spec_td4w = Spec(body=ca - cb)

    defs = [
        ("ANT_QPOS", spec_qpos, _ref_qpos),
        ("ANT_QSEL", spec_qsel, _ref_qsel),
        ("ANT_LERP", spec_lerp, _ref_lerp),
        ("ANT_NLERP", spec_nlerp, _ref_nlerp),
        ("ANT_TD4S", spec_td4s, _ref_td4s),
        ("ANT_TD4W", spec_td4w, _ref_td4w),
    ]
    replaceable = ["DVEHASH32L_SEED", "DVEHASH32L_BODY", "DVEHASH32L_DRAIN",
                   "BITWISE_XOR_TT_BV", "SHIFT_RIGHT_TT_BV",
                   "CODY_WAITE_CASCADE", "ADD_RANGE_WRAP"]
    for name, spec, ref in defs:
        if name in dops._SUB_OPCODE_FOR_NAME:
            _OPS[name] = next(o for o in dops.OPS if o.name == name)
            continue
        sha = {ver: lower(spec, ver=ver, perf_en=False, name=name).sha(ver)
               for ver in ("v3", "v4")}
        op = DveOp(name, spec, subdim=False, uops_sha=sha, reference=ref)
        if max(dops._SUB_OPCODE_FOR_NAME.values()) < 0x1F:
            dops.OPS.append(op)
            row = dops._CUSTOM_DVE_ROW_BASE + len(dops.OPS) - 1
        else:
            victim = replaceable.pop()
            i = next(i for i, o in enumerate(dops.OPS) if o.name == victim)
            row = dops._SUB_OPCODE_FOR_NAME.pop(victim)
            dops.OPS[i] = op
            dops.CUSTOM_DVE_SPECS.pop(victim, None)
            dops.CUSTOM_DVE_REFS.pop(victim, None)
        dops._SUB_OPCODE_FOR_NAME[name] = row
        dops.CUSTOM_DVE_SPECS[name] = spec
        dops.CUSTOM_DVE_REFS[name] = ref
        _OPS[name] = op
    return _OPS



COMPUTE = ("PE", "DVE", "ACT", "POOL", "SP")
DMAQ = {"SPD": ("SP", 12), "ACTD": ("ACT", 6), "POOLD": ("POOL", 12)}


class Emitter:
    def __init__(self, nc, es):
        self.nc = nc
        self.streams = list(COMPUTE)
        self.exec_eng = {e: e for e in COMPUTE}
        self.dma_rot = {}
        for q, (eng, nslots) in DMAQ.items():
            slots = [f"{q}{i}" for i in range(nslots)]
            self.dma_rot[q] = (nslots, slots)
            for s in slots:
                self.streams.append(s)
                self.exec_eng[s] = eng
        self.sems = {n: es.enter_context(nc.semaphore("sem_" + n))
                     for n in self.streams}
        self.count = dict.fromkeys(self.streams, 0)
        self.rotidx = dict.fromkeys(DMAQ, 0)
        self.clock = {e: dict.fromkeys(self.streams, 0) for e in COMPUTE}
        self.snap = {s: {0: dict.fromkeys(self.streams, 0)} for s in self.streams}
        self.regions = {}
        self.prog = {e: [] for e in COMPUTE}
        self.n_waits = 0
        self.n_ins = 0

    def _region(self, name):
        return self.regions.setdefault(name, {"w": None, "r": {}})

    def _wait(self, eng, stream, cnt):
        if cnt <= 0 or self.clock[eng].get(stream, 0) >= cnt:
            return
        self.prog[eng].append(("wait", stream, cnt))
        self.n_waits += 1
        snap = self.snap[stream].get(cnt)
        if snap is None:
            keys = [k for k in self.snap[stream] if k <= cnt]
            snap = self.snap[stream][max(keys)] if keys else {}
        c = self.clock[eng]
        c[stream] = max(c.get(stream, 0), cnt)
        for s, v in snap.items():
            if c.get(s, 0) < v:
                c[s] = v

    def emit(self, stream, fn, reads=(), writes=(), tag=None):
        if stream in DMAQ:
            nslots, slots = self.dma_rot[stream]
            slot = slots[self.rotidx[stream] % nslots]
            self.rotidx[stream] += 1
            stream = slot
            inc = 16
        else:
            inc = 1
        eng = self.exec_eng[stream]
        for rn in reads:
            r = self._region(rn)
            if r["w"] is not None:
                self._wait(eng, *r["w"])
        for wn in writes:
            r = self._region(wn)
            if r["w"] is not None:
                self._wait(eng, *r["w"])
            for rs, rc in r["r"].items():
                self._wait(eng, rs, rc)
        self.count[stream] += inc
        cnt = self.count[stream]
        self.prog[eng].append(("ins", fn, stream, inc, tag))
        self.n_ins += 1
        snapshot = dict(self.clock[eng])
        snapshot[stream] = cnt
        self.snap[stream][cnt] = snapshot
        for rn in reads:
            self._region(rn)["r"][stream] = cnt
        for wn in writes:
            r = self._region(wn)
            r["w"] = (stream, cnt)
            r["r"] = {}
        return cnt

    def merge_regions(self, srcs, dst):
        d = self._region(dst)
        for s in srcs:
            if s not in self.regions:
                continue
            r = self.regions[s]
            if r["w"] is not None:
                ws, wc = r["w"]
                d["r"][ws] = max(d["r"].get(ws, 0), wc)
            for rs, rc in r["r"].items():
                d["r"][rs] = max(d["r"].get(rs, 0), rc)

    def final_wait_all(self, eng="POOL"):
        for s in self.streams:
            if s != eng:
                self._wait(eng, s, self.count[s])

    def replay(self, eng, h):
        for item in self.prog[eng]:
            if item[0] == "wait":
                _, stream, cnt = item
                h.wait_ge(self.sems[stream], cnt)
            else:
                _, fn, stream, inc, tag = item
                ins = fn(h)
                ins.then_inc(self.sems[stream], inc)

    def install(self, block):
        @block.tensor
        def _(h):
            self.replay("PE", h)

        @block.vector
        def _(h):
            self.replay("DVE", h)

        @block.scalar
        def _(h):
            self.replay("ACT", h)

        @block.gpsimd
        def _(h):
            self.replay("POOL", h)

        @block.sync
        def _(h):
            self.replay("SP", h)



def _threefry2x32_pair(k1, k2, x0, x1):
    def rotl(x, d):
        return ((x << np.uint32(d)) | (x >> np.uint32(32 - d))).astype(np.uint32)
    rot = [[13, 15, 26, 6], [17, 29, 16, 24]]
    x0 = x0.astype(np.uint32).copy()
    x1 = x1.astype(np.uint32).copy()
    ks0, ks1 = np.uint32(k1), np.uint32(k2)
    ks2 = np.uint32(ks0 ^ ks1 ^ np.uint32(0x1BD11BDA))
    x0 = (x0 + ks0).astype(np.uint32)
    x1 = (x1 + ks1).astype(np.uint32)
    ks = [ks1, ks2, ks0, ks1, ks2, ks0]
    for i in range(5):
        for r in rot[i % 2]:
            x0 = (x0 + x1).astype(np.uint32)
            x1 = rotl(x1, r) ^ x0
        x0 = (x0 + ks[i]).astype(np.uint32)
        x1 = (x1 + np.uint32(ks[i + 1]) + np.uint32(i + 1)).astype(np.uint32)
    return x0, x1


def _dropout_masks():
    n = 16 * T * D
    x0 = np.zeros(n, np.uint32)
    x1 = np.arange(n, dtype=np.uint32)
    b0, b1 = _threefry2x32_pair(0, 42, x0, x1)
    bits32 = b0 ^ b1
    u32 = ((bits32 >> np.uint32(9)) | np.uint32(0x3F800000)).view(np.float32) - np.float32(1.0)
    keep32 = u32 < np.float32(0.8)
    bits64 = (b0.astype(np.uint64) << np.uint64(32)) | b1.astype(np.uint64)
    u64 = ((bits64 >> np.uint64(12)) | np.uint64(0x3FF0000000000000)).view(np.float64) - 1.0
    keep64 = u64 < np.float64(0.8)
    m32 = np.where(keep32, np.float32(0.625), np.float32(0.0))
    m64 = np.where(keep64, np.float32(0.625), np.float32(0.0))
    md = m64 - m32
    return (m32.reshape(16, T, D), md.reshape(16, T, D))


def _feature_major_mask(m):
    return np.ascontiguousarray(
        m.reshape(SEQ_PER_CORE, T, H, HD).transpose(0, 2, 3, 1))


def _host_consts():
    c = {}
    c["identb"] = np.eye(128).astype(ml_dtypes.bfloat16)
    c["identf"] = np.eye(128, dtype=np.float32)
    sel8 = np.zeros((8, 8 * 128), np.float32)
    for h in range(8):
        sel8[h, h * 128:(h + 1) * 128] = 1.0
    c["sel8"] = sel8
    e8 = np.zeros((128, 8 * 8), np.float32)
    for h in range(8):
        e8[:, h * 8 + h] = 1.0
    c["e8"] = e8.astype(ml_dtypes.bfloat16)
    iu = np.triu(np.ones((128, 128), np.float32), 1)
    c["trineg"] = np.tile(-iu, (1, 8)).astype(np.float32)
    inc = np.triu(np.ones((128, 128), np.float32), 0)
    c["trinc"] = np.tile(inc, (1, 8)).astype(np.float32)
    c["identr"] = np.tile(np.eye(128), (1, 8)).astype(ml_dtypes.bfloat16)
    return c


_CONST_CACHE = {}


def _get_host_consts():
    if not _CONST_CACHE:
        _CONST_CACHE.update(_host_consts())
        m32, md = _dropout_masks()
        _CONST_CACHE["m32"] = m32
        _CONST_CACHE["md"] = md
    return _CONST_CACHE


MSUB16 = float(1.5 * 2.0**28)
MSUB43 = float(1.5 * 2.0**43)
NEGBIG = float(-3.0e38)


class KB:
    def __init__(self, nc, es):
        self.nc = nc
        self.es = es
        self.em = Emitter(nc, es)
        self.ops = _register_ops()
        self._mk_dram()
        self._mk_sbuf()
        self._mk_psum()

    def _dram_in(self, name, shape, dt=F32):
        return self.nc.dram_tensor(name, list(shape), dt, kind="ExternalInput").ap()

    def _mk_dram(self):
        d = self.d = {}
        d["x"] = self._dram_in("x", (SEQ_PER_CORE, T, D))
        for w, sh in [("qw", (D, D)), ("kw", (D, D)), ("vw", (D, D)),
                      ("gw", (D, D)), ("ow", (D, D)), ("fcw", (D, 4 * D)),
                      ("pw", (4 * D, D)), ("w1", (4 * D, D)), ("w2", (D, 4 * D))]:
            d[w] = self._dram_in(w, sh)
        d["awbw"] = self._dram_in("awbw", (D, 16))
        d["b1r"] = self._dram_in("b1r", (128, 32))
        d["b2r"] = self._dram_in("b2r", (128, 8))
        d["ln1c"] = self._dram_in("ln1c", (128, 8))
        d["ln2c"] = self._dram_in("ln2c", (128, 8))
        d["onormc"] = self._dram_in("onormc", (128, 1))
        d["alog"] = self._dram_in("alog", (8, 1))
        d["dtb"] = self._dram_in("dtb", (8, 1))
        d["identb"] = self._dram_in("identb", (128, 128), BF16)
        d["identf"] = self._dram_in("identf", (128, 128))
        d["sel8"] = self._dram_in("sel8", (8, 8 * 128))
        d["e8"] = self._dram_in("e8", (128, 8 * 8), BF16)
        d["trineg"] = self._dram_in("trineg", (128, 8 * 128))
        d["trinc"] = self._dram_in("trinc", (128, 8 * 128))
        d["identr"] = self._dram_in("identr", (128, 8 * 128), BF16)
        d["m32f"] = self._dram_in("m32f", (SEQ_PER_CORE, 8, 128, T), BF16)
        d["mdf"] = self._dram_in("mdf", (SEQ_PER_CORE, 8, 128, T), BF16)
        d["y"] = self.nc.dram_tensor("y", [SEQ_PER_CORE, T, D], F32,
                                     kind="ExternalOutput").ap()
        sc = self.nc.dram_tensor
        d["s_x"] = sc("s_x", [D, T], F32).ap()
        for w, cols in [("s_qw", 8 * D), ("s_kw", 8 * D), ("s_vw", 8 * D),
                        ("s_gw", 8 * D), ("s_ow", 8 * D),
                        ("s_fcw", 8 * 4 * D), ("s_pw", 32 * D)]:
            d[w] = sc(w, [128, cols], BF16).ap()
        d["s_w1t"] = sc("s_w1t", [128, 8 * 4 * D], F8).ap()
        d["s_w2t"] = sc("s_w2t", [128, 32 * D], F8).ap()

    def _mk_sbuf(self):
        nc, es = self.nc, self.es
        t = self.t = {}

        def sb(name, shape, dt=F32):
            t[name] = es.enter_context(nc.sbuf_tensor("sb_" + name, list(shape), dt))
            return t[name]

        sb("identb", (128, 128), BF16)
        sb("identf", (128, 128))
        sb("sel8", (8, 8 * 128))
        sb("e8", (128, 8 * 8), BF16)
        sb("trineg", (128, 8 * 128))
        sb("trinc", (128, 8 * 128))
        sb("identr", (128, 8 * 128), BF16)
        sb("ones_row", (1, 128))
        sb("ones_colb", (128, 1), BF16)
        sb("ln1c", (128, 8))
        sb("ln2c", (128, 8))
        sb("onormc", (128, 1))
        sb("alog", (8, 1))
        sb("dtb", (8, 1))
        sb("nA", (8, 1))
        sb("awbw", (128, 8 * 16), BF16)
        for n in ("ind", "cv16", "cv4", "epsp", "epsn"):
            sb(n, (128, 1))
        sb("b1q", (128, 32))
        sb("b2q", (128, 8))
        sb("b1_8", (128, 32), F8)
        sb("b2_8", (128, 8), F8)
        sb("S0", (128, 8 * 128))
        sb("S0b", (128, 8 * 128), BF16)
        for n in ("r_mu", "r_inv", "r_s", "r_t1", "r_t2"):
            sb(n, (1, 512))
        for n in ("qn", "kn", "r_za", "r_zb", "r_sca", "r_scb",
                  "r_lb", "r_beta", "r_g", "r_glb"):
            sb(n, (8, 512))
        sb("gtile", (8, T))
        sb("gprev", (8, 1))
        sb("gz", (8, 1))
        for n in ("A0", "Qs", "A1", "Td"):
            sb(n, (128, 512), F16)
        sb("fscr0", (128, 1024))
        sb("fscr1", (128, 1024))
        sb("xbfs", (128, 4096), BF16)
        sb("x_sub", (128, 8 * 512))
        for n in ("mk0", "mk1", "mk2", "mk3"):
            sb(n, (128, 512), BF16)
        sb("wwin0", (128, 8192), BF16)
        sb("wwin1", (128, 8192), BF16)
        sb("gcolT", (128, 32))
        sb("gamc", (128, 8))
        sb("dcy", (128, 8))
        sb("ndcy", (128, 8))
        sb("gCb", (128, 8))
        sb("osums", (128, 8))
        sb("oscale", (128, 8))
        sb("qnT", (128, 32))
        base = 229344
        layout = [
            ("qsl", 0, 8192, BF16), ("ksl", 8192, 8192, BF16),
            ("vsl", 16384, 8192, BF16), ("gsl", 24576, 8192, BF16),
            ("hsl", 32768, 8192, BF16),
            ("fcsl", 0, 16384, BF16), ("a1sl", 16384, 8192, F32),
            ("h2sl", 24576, 4096, BF16),
            ("h8sl", 0, 16384, F8), ("u8sl", 16384, 4096, F8),
            ("f8w0", 20480, 8192, F8), ("f8w1", 28672, 8192, F8),
            ("o2f", 40960, 8192, BF16),
            ("A0b", 40960, 1024, F16), ("Qsb", 41984, 1024, F16),
            ("A1b", 43008, 1024, F16), ("Tdb", 44032, 1024, F16),
            ("fscr2", 45056, 2048, F32), ("fscr3", 47104, 2048, F32),
            ("Esl", 49152, 4096, F32), ("otok", 53248, 4096, F32),
            ("oscr", 57344, 4096, F32),
            ("Xa", 61440, 2048, BF16), ("XTa", 63488, 2048, BF16),
            ("Xb", 65536, 2048, BF16), ("XTb", 67584, 2048, BF16),
            ("Zb", 69632, 2048, BF16), ("Ab", 71680, 2048, BF16),
            ("Kts", 73728, 2048, BF16),
            ("pzA", 0, 4096, F32), ("pzB", 4096, 4096, F32),
            ("pscr0B", 8192, 4096, F32), ("pscr1B", 12288, 4096, F32),
            ("pwsB", 20480, 16384, F32),
        ]
        arena_bytes = 75776
        arena_base = base - arena_bytes
        self.arena_groups = {
            "ffn": ["h8sl", "u8sl", "f8w0", "f8w1", "A0b", "Qsb", "A1b",
                    "Tdb", "fscr2", "fscr3"],
            "attn_gen": ["qsl", "ksl", "vsl", "gsl", "hsl"],
            "post": ["fcsl", "a1sl", "h2sl"],
        }
        for name, off, nbytes, dt in layout:
            isz = {F32: 4, BF16: 2, F8: 1, F16: 2}[dt]
            t[name] = nc.alloc_sbuf_tensor_at(
                name, [128, nbytes // isz], dt, offset=arena_base + off)
        left = nc._sbuf_addr_for_side("left")
        assert left <= arena_base, (
            f"SBUF overflow: left ptr {left} > arena base {arena_base}")
        self.sbuf_slack = arena_base - left

    def _mk_psum(self):
        nc, es = self.nc, self.es
        p = self.p = {}
        for n in ("pb0", "pb1", "pb2"):
            p[n] = es.enter_context(nc.psum_tensor(n, [128, 512], F32))
        for n in ("pm0", "pm1"):
            p[n] = es.enter_context(nc.psum_tensor(n, [128, 1024], F32))
        p["ptb"] = es.enter_context(nc.psum_tensor("ptb", [128, 1024], BF16))

    def E(self, *a, **k):
        return self.em.emit(*a, **k)

    def cust(self, opname, out, in0, in1=None, s0=0.0, s1=0.0, imm2=0.0,
             reads=(), writes=()):
        op = self.ops[opname]
        kw = dict(out=out, in0=in0, s0=s0, s1=s1, imm2=imm2)
        if in1 is not None:
            kw["in1"] = in1
        self.E("DVE", lambda h: h._custom_dve(op, **kw),
               reads=reads, writes=writes)

    def sgn_q43(self, zin, zr, wout, wr, adj_w=False, scr=("fscr0", "fscr1")):
        t = self.t
        n = zin.shape[-1]
        n0, n1 = scr
        s0 = t[n0][:, 0:n]
        s1 = t[n1][:, 0:n]
        src, srcr = zin, zr
        if adj_w:
            self.cust("ANT_TD4W", s0, zin, s0=float(2.0**-10), s1=float(2.0**-4),
                      imm2=1.0, reads=[zr], writes=[n0])
            self.cust("ANT_NLERP", s1, zin, in1=s0, s0=t["epsp"][:, :],
                      reads=[zr, n0, "epsp"], writes=[n1])
            src, srcr = s1, n1
        self.cust("ANT_QPOS", s0, src, s0=t["cv4"][:, :], s1=MSUB43,
                  imm2=NEGBIG, reads=[srcr, "cv4"], writes=[n0])
        self.cust("ANT_QSEL", s0, src, in1=s0, s1=MSUB43, imm2=float(2.0**-12),
                  reads=[srcr, n0], writes=[n0])
        self.cust("ANT_LERP", wout, src, in1=s0, s0=t["ind"][:, :],
                  reads=[srcr, n0, "ind"], writes=[wr])

    def dma(self, q, out, in_, reads=(), writes=()):
        self.E(q, lambda h: h.dma_start(out=out, in_=in_),
               reads=reads, writes=writes)

    def enter_phase(self, grp):
        others = [n for g, names in self.arena_groups.items() if g != grp
                  for n in names]
        for name in self.arena_groups[grp]:
            self.em.merge_regions(others, name)

    def prep(self):
        t, d, p, E = self.t, self.d, self.p, self.E
        for n in ("identb", "identf", "sel8", "e8", "trineg", "trinc",
                  "identr", "ln1c", "ln2c", "onormc", "alog", "dtb"):
            self.dma("SPD", t[n][:, :], d[n][:, :], writes=[n])
        self.dma("SPD", t["fscr0"][:, 0:128].rearrange("p (a w) -> p a w", w=16),
                 bass.AP(d["awbw"].tensor, 0, [[16, 128], [128 * 16, 8], [1, 16]]),
                 writes=["fscr0"])
        E("DVE", lambda h: h.tensor_copy(out=t["awbw"][:, :], in_=t["fscr0"][:, 0:128]),
          reads=["fscr0"], writes=["awbw"])
        self.dma("SPD", t["b1q"][:, :], d["b1r"][:, :], writes=["b1q"])
        self.dma("SPD", t["b2q"][:, :], d["b2r"][:, :], writes=["b2q"])
        E("DVE", lambda h: h.memset(t["ones_row"][:, :], 1.0), writes=["ones_row"])
        E("DVE", lambda h: h.memset(t["ones_colb"][:, :], 1.0), writes=["ones_colb"])
        E("DVE", lambda h: h.memset(t["gz"][:, :], 0.0), writes=["gz"])
        E("DVE", lambda h: h.memset(t["ind"][:, :], 1.0), writes=["ind"])
        E("DVE", lambda h: h.tensor_scalar(out=t["ind"][:, :], in0=t["ind"][:, :],
                                           scalar1=float(2.0**-40), scalar2=1.0,
                                           op0=AL.add, op1=AL.subtract),
          reads=["ind"], writes=["ind"])
        E("DVE", lambda h: h.tensor_scalar(out=t["ind"][:, :], in0=t["ind"][:, :],
                                           scalar1=float(2.0**40), scalar2=None,
                                           op0=AL.mult),
          reads=["ind"], writes=["ind"])
        for n, base in (("cv16", 2.0**42), ("cv4", 2.0**49)):
            E("DVE", lambda h, n=n, base=base: h.memset(t[n][:, :], float(base)),
              writes=[n])
            E("DVE", lambda h, n=n: h.tensor_scalar(
                out=t[n][:, :], in0=t[n][:, :], scalar1=1.0, scalar2=None,
                op0=AL.add), reads=[n], writes=[n])
        E("DVE", lambda h: h.tensor_scalar(out=t["epsp"][:, :], in0=t["ind"][:, :],
                                           scalar1=float(2.0**-30), scalar2=None,
                                           op0=AL.mult),
          reads=["ind"], writes=["epsp"])
        E("DVE", lambda h: h.tensor_scalar(out=t["epsn"][:, :], in0=t["ind"][:, :],
                                           scalar1=float(-2.0**-30), scalar2=None,
                                           op0=AL.mult),
          reads=["ind"], writes=["epsn"])
        E("ACT", lambda h: h.activation(out=t["nA"][:, :], in_=t["alog"][:, :],
                                        func=AF.Exp), reads=["alog"], writes=["nA"])
        E("DVE", lambda h: h.tensor_scalar(out=t["nA"][:, :], in0=t["nA"][:, :],
                                           scalar1=-1.0, scalar2=None, op0=AL.mult),
          reads=["nA"], writes=["nA"])
        self.sgn_q43(t["b1q"][:, :], "b1q", t["b1_8"][:, :], "b1_8", adj_w=True)
        E("DVE", lambda h: h.tensor_copy(out=t["b1q"][:, :], in_=t["b1_8"][:, :]),
          reads=["b1_8"], writes=["b1q"])
        self.sgn_q43(t["b2q"][:, :], "b2q", t["b2_8"][:, :], "b2_8", adj_w=True)
        E("DVE", lambda h: h.tensor_copy(out=t["b2q"][:, :], in_=t["b2_8"][:, :]),
          reads=["b2_8"], writes=["b2q"])
        self.enter_phase("ffn")
        astages = []
        for w, scr in [("qw", "s_qw"), ("kw", "s_kw"), ("vw", "s_vw"),
                       ("gw", "s_gw"), ("ow", "s_ow"), ("fcw", "s_fcw"),
                       ("pw", "s_pw")]:
            src = d[w]
            rows, cols = src.shape
            nst = rows * cols // (128 * 4096)
            for i in range(nst):
                astages.append((w, src, d[scr], cols, 4096 // cols, i))

        def emit_astage(st, par):
            w, src, scr_ap, cols, spc, i = st
            if par == 0:
                buf, bufr = t["x_sub"], "x_sub"
                bf, bfr, bo = t["xbfs"], "xbfs", 0
            else:
                buf, bufr = t["pwsB"], "pwsB"
                bf, bfr, bo = t["wwin0"], "w0b", 4096
            in_ap = bass.AP(src.tensor, i * spc * 128 * cols,
                            [[cols, 128], [128 * cols, spc], [1, cols]])
            self.dma("SPD", buf[:, 0:4096].rearrange("p (a c) -> p a c", c=cols),
                     in_ap, writes=[bufr])
            E("ACT", lambda h, buf=buf, bf=bf, bo=bo: h.activation(
                out=bf[:, bo:bo + 4096], in_=buf[:, 0:4096], func=AF.Copy),
              reads=[bufr], writes=[bfr])
            if w == "pw":
                for sl in range(spc):
                    hc = i * spc + sl
                    out_ap = bass.AP(scr_ap.tensor, hc * 128,
                                     [[32768, 128], [4096, 8], [1, 128]])
                    self.dma("POOLD", out_ap,
                             bf[:, bo + sl * 1024:bo + (sl + 1) * 1024].rearrange(
                                 "p (a j) -> p a j", j=128),
                             reads=[bfr])
            else:
                self.dma("POOLD", scr_ap[:, i * 4096:(i + 1) * 4096],
                         bf[:, bo:bo + 4096], reads=[bfr])

        pieces = ([("w1", sl, 0) for sl in range(32)]
                  + [("w2", sl, hq) for sl in range(8) for hq in range(4)])
        ai = 0
        for k, (w, sl, cq) in enumerate(pieces):
            par = k % 2
            if par == 0:
                zin, zr = t["pzA"], "pzA"
                scrp = ("fscr0", "fscr1")
                wbf, wo, wreg = t["wwin0"], 0, "w0a"
                pt, ptregs = p["ptb"], ["ptb"]
            else:
                zin, zr = t["pzB"], "pzB"
                scrp = ("pscr0B", "pscr1B")
                wbf, wo, wreg = t["wwin1"], 0, "w1a"
                pt, ptregs = p["ptb"], ["ptb"]
            src = d[w]
            self.dma("SPD", zin[:, 0:1024],
                     src[sl * 128:(sl + 1) * 128, cq * 1024:(cq + 1) * 1024],
                     writes=[zr])
            u8q = t["u8sl"][:, par * 1024:(par + 1) * 1024]
            self.sgn_q43(zin[:, 0:1024], zr, u8q, f"pw8:{par}", adj_w=True,
                         scr=scrp)
            E("DVE", lambda h, u8q=u8q, wbf=wbf, wo=wo: h.tensor_copy(
                out=wbf[:, wo:wo + 1024], in_=u8q),
              reads=[f"pw8:{par}"], writes=[wreg])

            def trs(h, wbf=wbf, wo=wo, pt=pt):
                for dc in range(8):
                    ins = h.transpose(pt[:, dc * 128:(dc + 1) * 128],
                                      wbf[:, wo + dc * 128:wo + (dc + 1) * 128],
                                      t["identb"][:, :])
                return ins
            E("PE", trs, reads=[wreg, "identb"], writes=ptregs)
            u8o = t["u8sl"][:, (2 + par) * 1024:(3 + par) * 1024]
            E("DVE", lambda h, u8o=u8o, pt=pt: h.tensor_copy(
                out=u8o, in_=pt[:, 0:1024]),
              reads=ptregs, writes=[f"pw8:{2 + par}"])
            if w == "w1":
                out_ap = bass.AP(d["s_w1t"].tensor, sl * 128,
                                 [[32768, 128], [4096, 8], [1, 128]])
                st_src = u8o.rearrange("p (a j) -> p a j", j=128)
            else:
                out_ap = bass.AP(d["s_w2t"].tensor, sl * 4096 + cq * 1024,
                                 [[32768, 128], [1, 1024]])
                st_src = u8o
            self.dma("POOLD", out_ap, st_src, reads=[f"pw8:{2 + par}"])
            if k % 5 in (0, 2) and ai < len(astages):
                emit_astage(astages[ai], ai % 2)
                ai += 1
        while ai < len(astages):
            emit_astage(astages[ai], ai % 2)
            ai += 1
        self.em.merge_regions(["pzA", "pzB", "pscr0B", "pscr1B"], "h8sl")
        self.em.merge_regions(["pw8:0", "pw8:1", "pw8:2", "pw8:3"], "u8sl")
        self.em.merge_regions(["pwsB"], "f8w0")
        self.em.merge_regions(["pwsB"], "f8w1")

    def _sx_ap(self, sub):
        return bass.AP(self.d["s_x"].tensor, sub * 512,
                       [[2048, 128], [128 * 2048, 8], [1, 512]])

    def load_x_tile(self, seq):
        t, d, p, E = self.t, self.d, self.p, self.E
        for tc in range(16):
            self.dma("SPD", t["x_sub"][:, 0:1024],
                     d["x"][seq, tc * 128:(tc + 1) * 128, :], writes=["x_sub"])
            def trs(h):
                for dc in range(8):
                    ins = h.transpose(p["pm0"][:, dc * 128:(dc + 1) * 128],
                                      t["x_sub"][:, dc * 128:(dc + 1) * 128],
                                      t["identf"][:, :])
                return ins
            E("PE", trs, reads=["x_sub", "identf"],
              writes=[f"pm0:{i}" for i in range(8)])
            E("ACT", lambda h: h.activation(out=t["x_sub"][:, 1024:2048],
                                            in_=p["pm0"][:, :], func=AF.Copy),
              reads=[f"pm0:{i}" for i in range(8)], writes=["x_sub"])
            out_ap = bass.AP(d["s_x"].tensor, tc * 128,
                             [[2048, 128], [128 * 2048, 8], [1, 128]])
            self.dma("POOLD", out_ap,
                     t["x_sub"][:, 1024:2048].rearrange("p (a j) -> p a j", j=128),
                     reads=["x_sub"], writes=["s_x"])

    def store_x_tile(self, seq):
        t, d, p, E = self.t, self.d, self.p, self.E
        for tc in range(16):
            in_ap = bass.AP(d["s_x"].tensor, tc * 128,
                            [[2048, 128], [128 * 2048, 8], [1, 128]])
            self.dma("SPD", t["x_sub"][:, 0:1024].rearrange(
                "p (a j) -> p a j", j=128), in_ap,
                reads=["s_x"], writes=["x_sub"])
            def trs(h):
                for dc in range(8):
                    ins = h.transpose(p["pm0"][:, dc * 128:(dc + 1) * 128],
                                      t["x_sub"][:, dc * 128:(dc + 1) * 128],
                                      t["identf"][:, :])
                return ins
            E("PE", trs, reads=["x_sub", "identf"],
              writes=[f"pm0:{i}" for i in range(8)])
            E("ACT", lambda h: h.activation(out=t["x_sub"][:, 1024:2048],
                                            in_=p["pm0"][:, :], func=AF.Copy),
              reads=[f"pm0:{i}" for i in range(8)], writes=["x_sub"])
            self.dma("POOLD", d["y"][seq, tc * 128:(tc + 1) * 128, :],
                     t["x_sub"][:, 1024:2048], reads=["x_sub"], writes=["y"])

    def ln_stats(self, with_s=False):
        t, p, E = self.t, self.p, self.E
        E("ACT", lambda h: h.activation(out=t["xbfs"][:, :], in_=t["x_sub"][:, 0:4096],
                                        func=AF.Copy),
          reads=["x_sub"], writes=["xbfs"])
        def mmsum(h):
            for dc in range(8):
                ins = h.matmul(p["pb1"][0:1, :], t["ones_colb"][:, :],
                               t["xbfs"][:, dc * 512:(dc + 1) * 512],
                               start=(dc == 0), stop=(dc == 7))
            return ins
        E("PE", mmsum, reads=["xbfs", "ones_colb"], writes=["pb1"])
        E("ACT", lambda h: h.activation(out=t["xbfs"][:, :], in_=t["xbfs"][:, :],
                                        func=AF.Square),
          reads=["xbfs"], writes=["xbfs"])
        def mmsq(h):
            for dc in range(8):
                ins = h.matmul(p["pb2"][0:1, :], t["ones_colb"][:, :],
                               t["xbfs"][:, dc * 512:(dc + 1) * 512],
                               start=(dc == 0), stop=(dc == 7))
            return ins
        E("PE", mmsq, reads=["xbfs", "ones_colb"], writes=["pb2"])
        E("DVE", lambda h: h.tensor_scalar(out=t["r_mu"][:, :], in0=p["pb1"][0:1, :],
                                           scalar1=float(1.0 / 1024), scalar2=None,
                                           op0=AL.mult), reads=["pb1"], writes=["r_mu"])
        E("DVE", lambda h: h.tensor_scalar(out=t["r_t1"][:, :], in0=p["pb2"][0:1, :],
                                           scalar1=float(1.0 / 1024), scalar2=1e-5,
                                           op0=AL.mult, op1=AL.add),
          reads=["pb2"], writes=["r_t1"])
        E("DVE", lambda h: h.tensor_mul(out=t["r_t2"][:, :], in0=t["r_mu"][:, :],
                                        in1=t["r_mu"][:, :]),
          reads=["r_mu"], writes=["r_t2"])
        E("DVE", lambda h: h.tensor_sub(out=t["r_t1"][:, :], in0=t["r_t1"][:, :],
                                        in1=t["r_t2"][:, :]),
          reads=["r_t1", "r_t2"], writes=["r_t1"])
        E("ACT", lambda h: h.activation(out=t["r_t2"][:, :], in_=t["r_t1"][:, :],
                                        func=AF.Ln), reads=["r_t1"], writes=["r_t2"])
        E("ACT", lambda h: h.activation(out=t["r_inv"][:, :], in_=t["r_t2"][:, :],
                                        func=AF.Exp, scale=-0.5),
          reads=["r_t2"], writes=["r_inv"])
        if with_s:
            E("DVE", lambda h: h.reciprocal(out=t["r_t2"][:, :], in_=t["r_t1"][:, :]),
              reads=["r_t1"], writes=["r_t2"])
            E("DVE", lambda h: h.tensor_scalar(out=t["r_t2"][:, :], in0=t["r_t2"][:, :],
                                               scalar1=float(-1e-5),
                                               scalar2=float(1.0 + 1e-6),
                                               op0=AL.mult, op1=AL.add),
              reads=["r_t2"], writes=["r_t2"])
            E("ACT", lambda h: h.activation(out=t["r_t2"][:, :], in_=t["r_t2"][:, :],
                                            func=AF.Ln), reads=["r_t2"], writes=["r_t2"])
            E("ACT", lambda h: h.activation(out=t["r_t2"][:, :], in_=t["r_t2"][:, :],
                                            func=AF.Exp, scale=-0.5),
              reads=["r_t2"], writes=["r_t2"])
            E("DVE", lambda h: h.tensor_mul(out=t["r_s"][:, :], in0=t["r_inv"][:, :],
                                            in1=t["r_t2"][:, :]),
              reads=["r_inv", "r_t2"], writes=["r_s"])

    def bc8(self, name, col0=0, rep=128):
        ap = self.t[name][:, :]
        return bass.AP(ap.tensor, col0, [[ap.shape[1], 128], [1, 8], [0, rep]])

    def bcast_row(self, row_ap, psname, width=512, col0=0, reads=()):
        t, p, E = self.t, self.p, self.E
        if psname.startswith("pm"):
            wr = [f"{psname}:{i}" for i in
                  range(col0 // 128, (col0 + width + 127) // 128)]
        else:
            wr = [psname]
        E("PE", lambda h: h.matmul(p[psname][:, col0:col0 + width],
                                   t["ones_row"][:, :], row_ap,
                                   start=True, stop=True),
          reads=list(reads) + ["ones_row"], writes=wr)

    def ffn_half(self, seq):
        import os
        dbg = os.environ.get("ANT_DBG", "")
        t, d, p, E = self.t, self.d, self.p, self.E
        self.enter_phase("ffn")
        for sub in range(NSUB):
            self.dma("SPD", t["x_sub"][:, :].rearrange("p (a j) -> p a j", j=512),
                     self._sx_ap(sub), reads=["s_x"], writes=["x_sub"])
            self.ln_stats()
            self.bcast_row(t["r_mu"][:, :], "pb1", reads=["r_mu"])
            self.bcast_row(t["r_inv"][:, :], "pb2", reads=["r_inv"])
            for dc in range(8):
                xs = t["x_sub"][:, dc * 512:(dc + 1) * 512]
                fa, fb = (("fscr0", "fscr1") if dc % 2 == 0
                          else ("fscr2", "fscr3"))
                ta, tb = t[fa], t[fb]
                E("DVE", lambda h, xs=xs, ta=ta: h.tensor_sub(
                    out=ta[:, 0:512], in0=xs, in1=p["pb1"][:, :]),
                  reads=["x_sub", "pb1"], writes=[fa])
                E("DVE", lambda h, ta=ta, tb=tb: h.tensor_mul(
                    out=tb[:, 0:512], in0=ta[:, 0:512],
                    in1=p["pb2"][:, :]),
                  reads=[fa, "pb2"], writes=[fb])
                self.cust("ANT_QPOS", ta[:, 0:512], tb[:, 0:512],
                          s0=t["cv4"][:, :], s1=MSUB43, imm2=NEGBIG,
                          reads=[fb, "cv4"], writes=[fa])
                self.cust("ANT_LERP", t["u8sl"][:, dc * 512:(dc + 1) * 512],
                          tb[:, 0:512], in1=ta[:, 0:512],
                          s0=t["ind"][:, :],
                          reads=[fb, fa, "ind"], writes=["u8sl"])
            if dbg == "u8":
                E("DVE", lambda h: h.tensor_copy(out=t["x_sub"][:, :],
                                                 in_=t["u8sl"][:, :]),
                  reads=["u8sl"], writes=["x_sub"])
                self.dma("POOLD", self._sx_ap(sub),
                         t["x_sub"][:, :].rearrange("p (a j) -> p a j", j=512),
                         reads=["x_sub"], writes=["s_x"])
                continue
            for hq in range(4):
                w = f"f8w{hq % 2}"
                in_ap = bass.AP(d["s_w1t"].tensor, hq * 1024,
                                [[32768, 128], [4096, 8], [1, 1024]])
                self.dma("SPD", t[w][:, :].rearrange("p (a j) -> p a j", j=1024),
                         in_ap, writes=[w])
                for hcl in range(8):
                    hc = hq * 8 + hcl
                    pb = p[f"pb{hc % 2}"]
                    pbr = f"pb{hc % 2}"
                    def grp(h, w=w, hcl=hcl, pb=pb):
                        for dc in range(8):
                            ins = h.matmul(
                                pb[:, :],
                                t[w][:, dc * 1024 + hcl * 128:dc * 1024 + (hcl + 1) * 128],
                                t["u8sl"][:, dc * 512:(dc + 1) * 512],
                                start=(dc == 0), stop=(dc == 7))
                        return ins
                    E("PE", grp, reads=[w, "u8sl"], writes=[pbr])
                    sfx = "" if hc % 2 == 0 else "b"
                    rA0, rQs, rA1, rTd = ("A0" + sfx, "Qs" + sfx,
                                          "A1" + sfx, "Td" + sfx)
                    tA0, tQs, tA1, tTd = (t["A0" + sfx] if sfx else t["A0"],
                                          t["Qsb"] if sfx else t["Qs"],
                                          t["A1b"] if sfx else t["A1"],
                                          t["Tdb"] if sfx else t["Td"])
                    E("ACT", lambda h, pb=pb, hc=hc, tA0=tA0: h.activation(
                        out=tA0[:, :], in_=pb[:, :], func=AF.Relu,
                        bias=t["b1q"][:, hc:hc + 1]),
                      reads=[pbr, "b1q"], writes=[rA0])
                    self.cust("ANT_QPOS", tQs[:, :], tA0[:, :],
                              s0=t["cv16"][:, :], s1=MSUB16, imm2=float(2.0**-14),
                              reads=[rA0, "cv16"], writes=[rQs])
                    self.cust("ANT_LERP", tA1[:, :], tA0[:, :],
                              in1=tQs[:, :], s0=t["ind"][:, :],
                              reads=[rA0, rQs, "ind"], writes=[rA1])
                    self.cust("ANT_TD4S", tTd[:, :], tA1[:, :],
                              s0=float(2.0**-16), s1=float(2.0**-2), imm2=4.0,
                              reads=[rA1], writes=[rTd])
                    self.cust("ANT_NLERP", tA0[:, :], tA1[:, :],
                              in1=tTd[:, :], s0=t["epsn"][:, :],
                              reads=[rA1, rTd, "epsn"], writes=[rA0])
                    self.cust("ANT_QPOS", tQs[:, :], tA0[:, :],
                              s0=t["cv4"][:, :], s1=MSUB43, imm2=float(2.0**-6),
                              reads=[rA0, "cv4"], writes=[rQs])
                    self.cust("ANT_LERP", t["h8sl"][:, hc * 512:(hc + 1) * 512],
                              tA0[:, :], in1=tQs[:, :], s0=t["ind"][:, :],
                              reads=[rA0, rQs, "ind"], writes=["h8sl"])
            if dbg == "h8":
                E("DVE", lambda h: h.tensor_copy(out=t["x_sub"][:, :],
                                                 in_=t["h8sl"][:, 0:4096]),
                  reads=["h8sl"], writes=["x_sub"])
                self.dma("POOLD", self._sx_ap(sub),
                         t["x_sub"][:, :].rearrange("p (a j) -> p a j", j=512),
                         reads=["x_sub"], writes=["s_x"])
                continue
            for dc in range(8):
                w = f"f8w{dc % 2}"
                self.dma("SPD", t[w][:, 0:4096],
                         d["s_w2t"][:, dc * 4096:(dc + 1) * 4096], writes=[w])
                mk32, mkd = t[f"mk{2 * (dc % 2)}"], t[f"mk{2 * (dc % 2) + 1}"]
                r32, rd = f"mk{2 * (dc % 2)}", f"mk{2 * (dc % 2) + 1}"
                self.dma("SPD", mk32[:, :],
                         d["m32f"][seq, dc, :, sub * 512:(sub + 1) * 512],
                         writes=[r32])
                self.dma("SPD", mkd[:, :],
                         d["mdf"][seq, dc, :, sub * 512:(sub + 1) * 512],
                         writes=[rd])
                pb = p[f"pb{dc % 2}"]
                pbr = f"pb{dc % 2}"
                def grp2(h, w=w, pb=pb):
                    for hc in range(32):
                        ins = h.matmul(pb[:, :],
                                       t[w][:, hc * 128:(hc + 1) * 128],
                                       t["h8sl"][:, hc * 512:(hc + 1) * 512],
                                       start=(hc == 0), stop=(hc == 31))
                    return ins
                E("PE", grp2, reads=[w, "h8sl"], writes=[pbr])
                sfx = "" if dc % 2 == 0 else "b"
                rA0, rQs, rA1 = "A0" + sfx, "Qs" + sfx, "A1" + sfx
                tA0 = t["A0b"] if sfx else t["A0"]
                tQs = t["Qsb"] if sfx else t["Qs"]
                tA1 = t["A1b"] if sfx else t["A1"]
                fa = "fscr0" if dc % 2 == 0 else "fscr2"
                ta = t[fa]
                mblend = "Td" if dc % 2 == 0 else "Tdb"
                tmb = t[mblend]
                E("ACT", lambda h, pb=pb, dc=dc, tA0=tA0: h.activation(
                    out=tA0[:, :], in_=pb[:, :], func=AF.Identity,
                    bias=t["b2q"][:, dc:dc + 1], scale=1.0),
                  reads=[pbr, "b2q"], writes=[rA0])
                self.cust("ANT_QPOS", tQs[:, :], tA0[:, :],
                          s0=t["cv16"][:, :], s1=MSUB16, imm2=NEGBIG,
                          reads=[rA0, "cv16"], writes=[rQs])
                self.cust("ANT_LERP", tA1[:, :], tA0[:, :],
                          in1=tQs[:, :], s0=t["ind"][:, :],
                          reads=[rA0, rQs, "ind"], writes=[rA1])
                E("DVE", lambda h, mkd=mkd, mk32=mk32, tmb=tmb: h.scalar_tensor_tensor(
                    out=tmb[:, :], in0=mkd[:, :], scalar=t["ind"][:, :],
                    in1=mk32[:, :], op0=AL.mult, op1=AL.add),
                  reads=[rd, r32, "ind"], writes=[mblend])
                E("POOL", lambda h, ta=ta, tA1=tA1, tmb=tmb: h.tensor_mul(
                    out=ta[:, 0:512], in0=tA1[:, :], in1=tmb[:, :]),
                  reads=[rA1, mblend], writes=[fa])
                E("POOL", lambda h, dc=dc, ta=ta: h.tensor_add(
                    out=t["x_sub"][:, dc * 512:(dc + 1) * 512],
                    in0=t["x_sub"][:, dc * 512:(dc + 1) * 512],
                    in1=ta[:, 0:512]),
                  reads=["x_sub", fa], writes=["x_sub"])
            self.dma("POOLD", self._sx_ap(sub),
                     t["x_sub"][:, :].rearrange("p (a j) -> p a j", j=512),
                     reads=["x_sub"], writes=["s_x"])

    def _win(self, full=True):
        if not hasattr(self, "_wslot"):
            self._wslot = 0
        if full:
            if self._wslot % 2:
                self._wslot += 1
            s = (self._wslot // 2) % 2
            self._wslot += 2
            return self.t[f"wwin{s}"], 0, [f"w{s}a", f"w{s}b"]
        s = self._wslot % 4
        self._wslot += 1
        return self.t[f"wwin{s // 2}"], (s % 2) * 4096, [f"w{s // 2}{'ab'[s % 2]}"]

    def attn(self, seq):
        import os
        asub = os.environ.get("ANT_ATTN", "full")
        t, d, p, E = self.t, self.d, self.p, self.E
        E("DVE", lambda h: h.memset(t["S0"][:, :], 0.0), writes=["S0"])
        E("DVE", lambda h: h.memset(t["S0b"][:, :], 0.0), writes=["S0b"])
        E("DVE", lambda h: h.memset(t["gprev"][:, :], 0.0), writes=["gprev"])
        for sub in range(NSUB):
            self.enter_phase("attn_gen")
            self.dma("SPD", t["x_sub"][:, :].rearrange("p (a j) -> p a j", j=512),
                     self._sx_ap(sub), reads=["s_x"], writes=["x_sub"])
            self.ln_stats(with_s=True)
            self.bcast_row(t["r_mu"][:, :], "pb1", reads=["r_mu"])
            self.bcast_row(t["r_s"][:, :], "pb2", reads=["r_s"])
            for dc in range(8):
                xs = t["x_sub"][:, dc * 512:(dc + 1) * 512]
                if dc % 2 == 0:
                    s0, s0r = t["fscr0"][:, 0:512], "fscr0"
                    s1, s1r = t["fscr1"][:, 0:512], "fscr1"
                else:
                    s0, s0r = t["Esl"][:, 0:512], "Esl"
                    s1, s1r = t["Esl"][:, 512:1024], "Esl"
                E("DVE", lambda h, xs=xs, s0=s0: h.tensor_sub(
                    out=s0, in0=xs, in1=p["pb1"][:, :]),
                  reads=["x_sub", "pb1"], writes=[s0r])
                E("DVE", lambda h, s0=s0, s1=s1: h.tensor_mul(
                    out=s1, in0=s0, in1=p["pb2"][:, :]),
                  reads=[s0r, "pb2"], writes=[s1r])
                E("ACT", lambda h, dc=dc, s1=s1: h.activation(
                    out=t["hsl"][:, dc * 512:(dc + 1) * 512],
                    in_=s1, func=AF.Identity,
                    scale=t["ln1c"][:, dc:dc + 1]),
                  reads=[s1r, "ln1c"], writes=["hsl"])
            if asub == "h":
                continue
            for unit, slab in (("s_qw", "qsl"), ("s_kw", "ksl"),
                               ("s_vw", "vsl"), ("s_gw", "gsl")):
                win, off, wr = self._win(full=True)
                self.dma("SPD", win[:, :], d[unit][:, :], writes=wr)
                for oc in range(8):
                    pb = p[f"pb{oc % 2}"]
                    pbr = f"pb{oc % 2}"
                    def grp(h, win=win, oc=oc, pb=pb):
                        for dc in range(8):
                            ins = h.matmul(
                                pb[:, :],
                                win[:, dc * 1024 + oc * 128:dc * 1024 + (oc + 1) * 128],
                                t["hsl"][:, dc * 512:(dc + 1) * 512],
                                start=(dc == 0), stop=(dc == 7))
                        return ins
                    E("PE", grp, reads=wr + ["hsl"], writes=[pbr])
                    E("ACT", lambda h, pb=pb, slab=slab, oc=oc: h.activation(
                        out=t[slab][:, oc * 512:(oc + 1) * 512], in_=pb[:, :],
                        func=AF.Silu), reads=[pbr], writes=[slab])
            E("ACT", lambda h: h.activation(
                out=t["gsl"][:, :], in_=t["gsl"][:, :], func=AF.Identity,
                scale=t["onormc"][:, :]),
              reads=["gsl", "onormc"], writes=["gsl"])
            if asub == "proj":
                continue
            for part, (c0, outr) in (("a", (0, "r_za")), ("b", (8, "r_zb"))):
                def grpab(h, c0=c0):
                    for dc in range(8):
                        ins = h.matmul(
                            p["pb0"][0:8, :],
                            t["awbw"][:, dc * 16 + c0:dc * 16 + c0 + 8],
                            t["hsl"][:, dc * 512:(dc + 1) * 512],
                            start=(dc == 0), stop=(dc == 7))
                    return ins
                E("PE", grpab, reads=["awbw", "hsl"], writes=["pb0"])
                if part == "a":
                    E("ACT", lambda h, outr=outr: h.activation(
                        out=t[outr][:, :], in_=p["pb0"][0:8, :], func=AF.Identity,
                        bias=t["dtb"][:, :]), reads=["pb0", "dtb"], writes=[outr])
                else:
                    E("ACT", lambda h, outr=outr: h.activation(
                        out=t[outr][:, :], in_=p["pb0"][0:8, :], func=AF.Copy),
                      reads=["pb0"], writes=[outr])
            for slab, row in (("qsl", "qn"), ("ksl", "kn")):
                E("DVE", lambda h, slab=slab: h.tensor_mul(
                    out=t["xbfs"][:, :], in0=t[slab][:, :], in1=t[slab][:, :]),
                  reads=[slab], writes=["xbfs"])
                def grpe8(h):
                    for hh in range(8):
                        ins = h.matmul(p["pb2"][0:8, :], t["e8"][:, hh * 8:(hh + 1) * 8],
                                       t["xbfs"][:, hh * 512:(hh + 1) * 512],
                                       start=(hh == 0), stop=(hh == 7))
                    return ins
                E("PE", grpe8, reads=["xbfs", "e8"], writes=["pb2"])
                E("DVE", lambda h, row=row: h.tensor_scalar(
                    out=t[row][:, :], in0=p["pb2"][0:8, :], scalar1=float(1e-6),
                    scalar2=None, op0=AL.add), reads=["pb2"], writes=[row])
                E("ACT", lambda h, row=row: h.activation(
                    out=t[row][:, :], in_=t[row][:, :], func=AF.Ln),
                  reads=[row], writes=[row])
                E("ACT", lambda h, row=row: h.activation(
                    out=t[row][:, :], in_=t[row][:, :], func=AF.Exp, scale=-0.5),
                  reads=[row], writes=[row])
            for hh in range(8):
                E("PE", lambda h, hh=hh: h.matmul(
                    p["pb1"][:, :], t["sel8"][:, hh * 128:(hh + 1) * 128],
                    t["kn"][:, :], start=True, stop=True),
                  reads=["sel8", "kn"], writes=["pb1"])
                E("DVE", lambda h, hh=hh: h.tensor_mul(
                    out=t["ksl"][:, hh * 512:(hh + 1) * 512],
                    in0=t["ksl"][:, hh * 512:(hh + 1) * 512], in1=p["pb1"][:, :]),
                  reads=["ksl", "pb1"], writes=["ksl"])
            def qnt(h):
                for c in range(4):
                    ins = h.transpose(p["pm0"][:, c * 8:(c + 1) * 8],
                                      t["qn"][0:8, c * 128:(c + 1) * 128],
                                      t["identf"][0:8, 0:8])
                return ins
            E("PE", qnt, reads=["qn", "identf"], writes=["pm0:0"])
            E("ACT", lambda h: h.activation(out=t["qnT"][:, :],
                                            in_=p["pm0"][:, 0:32], func=AF.Copy),
              reads=["pm0:0"], writes=["qnT"])
            E("ACT", lambda h: h.activation(out=t["r_beta"][:, :], in_=t["r_zb"][:, :],
                                            func=AF.Sigmoid),
              reads=["r_zb"], writes=["r_beta"])
            for src, dst, extra in (("r_za", "r_g", "softplus"),
                                    ("r_zb", "r_lb", "logsig")):
                E("ACT", lambda h, src=src: h.activation(
                    out=t["r_sca"][:, :], in_=t[src][:, :], func=AF.Abs),
                  reads=[src], writes=["r_sca"])
                E("ACT", lambda h: h.activation(out=t["r_sca"][:, :],
                                                in_=t["r_sca"][:, :], func=AF.Exp,
                                                scale=-1.0),
                  reads=["r_sca"], writes=["r_sca"])
                E("ACT", lambda h: h.activation(out=t["r_sca"][:, :],
                                                in_=t["r_sca"][:, :], func=AF.Ln,
                                                bias=1.0),
                  reads=["r_sca"], writes=["r_sca"])
                if extra == "softplus":
                    pass
                if extra == "softplus":
                    E("DVE", lambda h, src=src: h.tensor_scalar(
                        out=t["r_scb"][:, :], in0=t[src][:, :], scalar1=0.0,
                        scalar2=None, op0=AL.max), reads=[src], writes=["r_scb"])
                    E("DVE", lambda h: h.tensor_add(out=t["r_g"][:, :],
                                                    in0=t["r_sca"][:, :],
                                                    in1=t["r_scb"][:, :]),
                      reads=["r_sca", "r_scb"], writes=["r_g"])
                    E("DVE", lambda h: h.tensor_scalar(
                        out=t["r_g"][:, :], in0=t["r_g"][:, :],
                        scalar1=t["nA"][:, :], scalar2=None, op0=AL.mult),
                      reads=["r_g", "nA"], writes=["r_g"])
                else:
                    E("DVE", lambda h, src=src: h.tensor_scalar(
                        out=t["r_scb"][:, :], in0=t[src][:, :], scalar1=0.0,
                        scalar2=None, op0=AL.min), reads=[src], writes=["r_scb"])
                    E("DVE", lambda h: h.tensor_sub(out=t["r_lb"][:, :],
                                                    in0=t["r_scb"][:, :],
                                                    in1=t["r_sca"][:, :]),
                      reads=["r_scb", "r_sca"], writes=["r_lb"])
            E("DVE", lambda h, sub=sub: h.tensor_tensor_scan(
                out=t["gtile"][0:8, sub * 512:(sub + 1) * 512],
                data0=t["r_g"][:, :], data1=t["r_g"][:, :],
                initial=t["gprev"][0:8, 0:1], op0=AL.add, op1=AL.bypass),
              reads=["r_g", "gprev"], writes=["gtile"])
            E("DVE", lambda h, sub=sub: h.tensor_copy(
                out=t["gprev"][0:8, 0:1],
                in_=t["gtile"][0:8, sub * 512 + 511:sub * 512 + 512]),
              reads=["gtile"], writes=["gprev"])
            E("DVE", lambda h, sub=sub: h.tensor_add(
                out=t["r_glb"][:, :], in0=t["gtile"][0:8, sub * 512:(sub + 1) * 512],
                in1=t["r_lb"][:, :]), reads=["gtile", "r_lb"], writes=["r_glb"])
            if asub == "rows":
                continue
            nch = NCH_SUB if not asub.startswith("mixer") else int(asub[5:])
            for c in range(min(nch, NCH_SUB)):
                self._mixer_chunk(seq, sub, c)
            if asub.startswith("mixer"):
                continue
            if os.environ.get("ANT_DBG", "") in ("X0", "TRI", "ESL", "MT",
                                                 "Ymat", "Rmat", "Omat"):
                continue
            if os.environ.get("ANT_DBG", "") == "o2f":
                E("DVE", lambda h: h.tensor_copy(out=t["x_sub"][:, :],
                                                 in_=t["o2f"][:, :]),
                  reads=["o2f"], writes=["x_sub"])
                self.dma("POOLD", self._sx_ap(sub),
                         t["x_sub"][:, :].rearrange("p (a j) -> p a j", j=512),
                         reads=["x_sub"], writes=["s_x"])
                continue
            self._post_stage(seq, sub)
            self.dma("POOLD", self._sx_ap(sub),
                     t["x_sub"][:, :].rearrange("p (a j) -> p a j", j=512),
                     reads=["x_sub"], writes=["s_x"])

    def _mixer_chunk(self, seq, sub, c):
        import os
        amx = os.environ.get("ANT_MIX", "full")
        t, d, p, E = self.t, self.d, self.p, self.E
        dbg = os.environ.get("ANT_DBG", "")
        def dump(slab, reads):
            if seq == 0 and sub == 0 and c == 0:
                E("DVE", lambda h: h.tensor_copy(out=t["x_sub"][:, 0:1024],
                                                 in_=t[slab][:, :]),
                  reads=reads, writes=["x_sub"])
                self.dma("POOLD", self._sx_ap(0),
                         t["x_sub"][:, :].rearrange("p (a j) -> p a j", j=512),
                         reads=["x_sub"], writes=["s_x"])
        tok0 = sub * 512 + c * 128
        PM0 = [f"pm0:{i}" for i in range(8)]
        PM1 = [f"pm1:{i}" for i in range(8)]
        base = t["gz"][0:8, 0:1] if tok0 == 0 else t["gtile"][0:8, tok0 - 1:tok0]
        baser = "gz" if tok0 == 0 else "gtile"
        E("DVE", lambda h: h.tensor_scalar(
            out=t["r_sca"][0:8, 0:128], in0=t["gtile"][0:8, tok0:tok0 + 128],
            scalar1=base, scalar2=None, op0=AL.subtract),
          reads=["gtile", baser], writes=["r_sca"])
        E("DVE", lambda h, c=c: h.tensor_scalar(
            out=t["r_scb"][0:8, 0:128], in0=t["r_glb"][0:8, c * 128:(c + 1) * 128],
            scalar1=base, scalar2=None, op0=AL.subtract),
          reads=["r_glb", baser], writes=["r_scb"])
        def trg(h, c=c):
            h.transpose(p["pm0"][:, 0:8], t["r_sca"][0:8, 0:128],
                        t["identf"][0:8, 0:8])
            h.transpose(p["pm0"][:, 8:16], t["r_scb"][0:8, 0:128],
                        t["identf"][0:8, 0:8])
            h.transpose(p["pm0"][:, 16:24], t["r_beta"][0:8, c * 128:(c + 1) * 128],
                        t["identf"][0:8, 0:8])
            return h.transpose(p["pm0"][0:1, 24:32], t["r_sca"][0:8, 127:128],
                               t["identf"][0:8, 0:8])
        E("PE", trg, reads=["r_sca", "r_scb", "r_beta", "identf"], writes=["pm0:0"])
        E("ACT", lambda h: h.activation(out=t["gcolT"][:, :], in_=p["pm0"][:, 0:32],
                                        func=AF.Copy),
          reads=["pm0:0"], writes=["gcolT"])
        E("ACT", lambda h: h.activation(out=t["gamc"][:, :], in_=t["gcolT"][:, 0:8],
                                        func=AF.Exp), reads=["gcolT"], writes=["gamc"])
        self.bcast_row(t["gcolT"][0:1, 24:32], "pm1", width=8, col0=0,
                       reads=["gcolT"])
        E("DVE", lambda h: h.tensor_sub(out=t["dcy"][:, :], in0=p["pm1"][:, 0:8],
                                        in1=t["gcolT"][:, 0:8]),
          reads=["pm1:0", "gcolT"], writes=["dcy"])
        E("ACT", lambda h: h.activation(out=t["dcy"][:, :], in_=t["dcy"][:, :],
                                        func=AF.Exp), reads=["dcy"], writes=["dcy"])
        E("ACT", lambda h: h.activation(out=t["gCb"][:, :], in_=p["pm1"][:, 0:8],
                                        func=AF.Exp), reads=["pm1:0"], writes=["gCb"])
        E("DVE", lambda h: h.tensor_mul(out=t["ndcy"][:, :], in0=t["gamc"][:, :],
                                        in1=t["gcolT"][:, 16:24]),
          reads=["gamc", "gcolT"], writes=["ndcy"])
        E("DVE", lambda h: h.tensor_scalar(out=t["ndcy"][:, :], in0=t["ndcy"][:, :],
                                           scalar1=-1.0, scalar2=None, op0=AL.mult),
          reads=["ndcy"], writes=["ndcy"])
        if amx == "g":
            return

        def ch(slab, hh):
            return t[slab][:, hh * 512 + c * 128:hh * 512 + (c + 1) * 128]

        def bc_rows(h, src):
            for hh in range(8):
                ins = h.matmul(p["pm1"][:, hh * 128:(hh + 1) * 128],
                               t["sel8"][:, hh * 128:(hh + 1) * 128],
                               src, start=True, stop=True)
            return ins
        E("PE", lambda h: bc_rows(h, t["r_scb"][0:8, 0:128]),
          reads=["sel8", "r_scb"], writes=PM1)
        gb0 = self.bc8("gcolT", 0)
        E("DVE", lambda h: h.tensor_sub(out=t["Esl"][:, :], in0=p["pm1"][:, :],
                                        in1=gb0),
          reads=PM1 + ["gcolT"], writes=["Esl"])
        E("DVE", lambda h: h.tensor_scalar(out=t["Esl"][:, :], in0=t["Esl"][:, :],
                                           scalar1=0.0, scalar2=None, op0=AL.min),
          reads=["Esl"], writes=["Esl"])
        E("ACT", lambda h: h.activation(out=t["Esl"][:, :], in_=t["Esl"][:, :],
                                        func=AF.Exp), reads=["Esl"], writes=["Esl"])
        E("DVE", lambda h: h.tensor_mul(out=t["Esl"][:, :], in0=t["Esl"][:, :],
                                        in1=t["trineg"][:, :]),
          reads=["Esl", "trineg"], writes=["Esl"])
        if dbg == "ESL":
            dump("Esl", ["Esl"])
            return
        def kkmm(h):
            for hh in range(8):
                ins = h.matmul(p["pm0"][:, hh * 128:(hh + 1) * 128],
                               ch("ksl", hh), ch("ksl", hh), start=True, stop=True)
            return ins
        E("PE", kkmm, reads=["ksl"], writes=PM0)
        E("DVE", lambda h: h.tensor_mul(out=t["Xa"][:, :], in0=t["Esl"][:, :],
                                        in1=p["pm0"][:, :]),
          reads=["Esl"] + PM0, writes=["Xa"])
        if dbg == "X0":
            dump("Xa", ["Xa"])
            return
        if dbg == "TRI":
            dump("trineg", ["trineg"])
            return
        if amx == "gx":
            return
        E("PE", lambda h: bc_rows(h, t["r_sca"][0:8, 0:128]),
          reads=["sel8", "r_sca"], writes=PM1)
        E("DVE", lambda h: h.tensor_sub(out=t["Esl"][:, :], in0=p["pm1"][:, :],
                                        in1=gb0),
          reads=PM1 + ["gcolT"], writes=["Esl"])
        E("DVE", lambda h: h.tensor_scalar(out=t["Esl"][:, :], in0=t["Esl"][:, :],
                                           scalar1=0.0, scalar2=None, op0=AL.min),
          reads=["Esl"], writes=["Esl"])
        E("ACT", lambda h: h.activation(out=t["Esl"][:, :], in_=t["Esl"][:, :],
                                        func=AF.Exp), reads=["Esl"], writes=["Esl"])
        E("DVE", lambda h: h.tensor_mul(out=t["Esl"][:, :], in0=t["Esl"][:, :],
                                        in1=t["trinc"][:, :]),
          reads=["Esl", "trinc"], writes=["Esl"])
        def qkmm(h):
            for hh in range(8):
                ins = h.matmul(p["pm0"][:, hh * 128:(hh + 1) * 128],
                               ch("ksl", hh), ch("qsl", hh), start=True, stop=True)
            return ins
        E("PE", qkmm, reads=["ksl", "qsl"], writes=PM0)
        E("DVE", lambda h: h.tensor_mul(out=t["Ab"][:, :], in0=t["Esl"][:, :],
                                        in1=p["pm0"][:, :]),
          reads=["Esl"] + PM0, writes=["Ab"])
        if amx == "ab":
            return
        def xt0(h):
            for hh in range(8):
                ins = h.transpose(p["ptb"][:, hh * 128:(hh + 1) * 128],
                                  t["Xa"][:, hh * 128:(hh + 1) * 128],
                                  t["identb"][:, :])
            return ins
        E("PE", xt0, reads=["Xa", "identb"], writes=["ptb"])
        E("DVE", lambda h: h.tensor_copy(out=t["XTa"][:, :], in_=p["ptb"][:, :]),
          reads=["ptb"], writes=["XTa"])
        E("DVE", lambda h: h.tensor_add(out=t["Zb"][:, :], in0=t["Xa"][:, :],
                                        in1=t["identr"][:, :]),
          reads=["Xa", "identr"], writes=["Zb"])
        if amx == "mt0":
            return
        NLEV = 3
        names = {"X": "Xa", "XT": "XTa", "Xn": "Xb", "XTn": "XTb"}
        for lev in range(1, NLEV + 1):
            last = lev == NLEV
            X, XT, Xn, XTn = names["X"], names["XT"], names["Xn"], names["XTn"]
            if not last:
                def sq1(h, X=X, XT=XT):
                    for hh in range(8):
                        ins = h.matmul(p["pm0"][:, hh * 128:(hh + 1) * 128],
                                       t[XT][:, hh * 128:(hh + 1) * 128],
                                       t[X][:, hh * 128:(hh + 1) * 128],
                                       start=True, stop=True)
                    return ins
                E("PE", sq1, reads=[X, XT], writes=PM0)
            def sq2(h, X=X, XT=XT):
                for hh in range(8):
                    ins = h.matmul(p["pm1"][:, hh * 128:(hh + 1) * 128],
                                   t[X][:, hh * 128:(hh + 1) * 128],
                                   t[XT][:, hh * 128:(hh + 1) * 128],
                                   start=True, stop=True)
                return ins
            E("PE", sq2, reads=[X, XT], writes=PM1)
            if not last:
                E("ACT", lambda h, Xn=Xn: h.activation(out=t[Xn][:, :],
                                                       in_=p["pm0"][:, :],
                                                       func=AF.Copy),
                  reads=PM0, writes=[Xn])
            E("DVE", lambda h, XTn=XTn: h.tensor_copy(out=t[XTn][:, :],
                                                      in_=p["pm1"][:, :]),
              reads=PM1, writes=[XTn])
            def mtmm(h, XTn=XTn):
                for hh in range(8):
                    h.matmul(p["pm0"][:, hh * 128:(hh + 1) * 128],
                             t[XTn][:, hh * 128:(hh + 1) * 128],
                             t["Zb"][:, hh * 128:(hh + 1) * 128],
                             start=True, stop=False)
                h.matmul(p["pm0"][:, 0:512], t["identb"][:, :], t["Zb"][:, 0:512],
                         start=False, stop=False)
                return h.matmul(p["pm0"][:, 512:1024], t["identb"][:, :],
                                t["Zb"][:, 512:1024], start=False, stop=True)
            E("PE", mtmm, reads=[XTn, "Zb", "identb"], writes=PM0)
            E("ACT", lambda h: h.activation(out=t["Zb"][:, :], in_=p["pm0"][:, :],
                                            func=AF.Copy), reads=PM0, writes=["Zb"])
            names["X"], names["Xn"] = names["Xn"], names["X"]
            names["XT"], names["XTn"] = names["XTn"], names["XT"]
        if dbg == "MT":
            dump("Zb", ["Zb"])
            return
        if amx == "sq":
            return
        def ks0(h):
            for hh in range(8):
                ins = h.matmul(p["pm1"][:, hh * 128:(hh + 1) * 128],
                               ch("ksl", hh), t["S0b"][:, hh * 128:(hh + 1) * 128],
                               start=True, stop=True)
            return ins
        E("PE", ks0, reads=["ksl", "S0b"], writes=PM1)
        def vtr(h):
            for hh in range(8):
                ins = h.transpose(p["ptb"][:, hh * 128:(hh + 1) * 128],
                                  ch("vsl", hh), t["identb"][:, :])
            return ins
        E("PE", vtr, reads=["vsl", "identb"], writes=["ptb"])
        E("DVE", lambda h: h.tensor_mul(out=t["oscr"][:, :], in0=p["pm1"][:, :],
                                        in1=self.bc8("ndcy")),
          reads=PM1 + ["ndcy"], writes=["oscr"])
        E("DVE", lambda h: h.tensor_mul(out=t["otok"][:, :], in0=p["ptb"][:, :],
                                        in1=self.bc8("gcolT", 16)),
          reads=["ptb", "gcolT"], writes=["otok"])
        E("POOL", lambda h: h.tensor_add(out=t["Xa"][:, :], in0=t["otok"][:, :],
                                         in1=t["oscr"][:, :]),
          reads=["otok", "oscr"], writes=["Xa"])
        if dbg == "Ymat":
            dump("Xa", ["Xa"])
            return
        if amx == "y":
            return
        def rmm(h):
            for hh in range(8):
                ins = h.matmul(p["pm0"][:, hh * 128:(hh + 1) * 128],
                               t["Zb"][:, hh * 128:(hh + 1) * 128],
                               t["Xa"][:, hh * 128:(hh + 1) * 128],
                               start=True, stop=True)
            return ins
        E("PE", rmm, reads=["Zb", "Xa"], writes=PM0)
        E("ACT", lambda h: h.activation(out=t["Xb"][:, :], in_=p["pm0"][:, :],
                                        func=AF.Copy), reads=PM0, writes=["Xb"])
        if dbg == "Rmat":
            dump("Xb", ["Xb"])
            return
        if amx == "r":
            return
        def omm(h):
            for hh in range(8):
                ins = h.matmul(p["pm0"][:, hh * 128:(hh + 1) * 128],
                               t["Ab"][:, hh * 128:(hh + 1) * 128],
                               t["Xb"][:, hh * 128:(hh + 1) * 128],
                               start=True, stop=True)
            return ins
        E("PE", omm, reads=["Ab", "Xb"], writes=PM0)
        def qs0(h):
            for hh in range(8):
                ins = h.matmul(p["pm1"][:, hh * 128:(hh + 1) * 128],
                               ch("qsl", hh), t["S0b"][:, hh * 128:(hh + 1) * 128],
                               start=True, stop=True)
            return ins
        E("PE", qs0, reads=["qsl", "S0b"], writes=PM1)
        E("DVE", lambda h: h.tensor_mul(out=t["oscr"][:, :], in0=p["pm1"][:, :],
                                        in1=self.bc8("gamc")),
          reads=PM1 + ["gamc"], writes=["oscr"])
        E("DVE", lambda h: h.tensor_add(out=t["otok"][:, :], in0=t["oscr"][:, :],
                                        in1=p["pm0"][:, :]),
          reads=["oscr"] + PM0, writes=["otok"])
        E("DVE", lambda h, c=c: h.tensor_mul(out=t["otok"][:, :],
                                             in0=t["otok"][:, :],
                                             in1=self.bc8("qnT", c * 8)),
          reads=["otok", "qnT"], writes=["otok"])
        for hh in range(8):
            E("DVE", lambda h, hh=hh: h.scalar_tensor_tensor(
                out=t["oscr"][:, hh * 128:(hh + 1) * 128],
                in0=t["otok"][:, hh * 128:(hh + 1) * 128], scalar=1.0,
                in1=t["otok"][:, hh * 128:(hh + 1) * 128],
                op0=AL.bypass, op1=AL.mult,
                accum_out=t["osums"][:, hh:hh + 1]),
              reads=["otok"], writes=["oscr", "osums"])
        E("DVE", lambda h: h.tensor_scalar(
            out=t["osums"][:, :], in0=t["osums"][:, :],
            scalar1=float(1.0 / 128), scalar2=float(1e-6),
            op0=AL.mult, op1=AL.add), reads=["osums"], writes=["osums"])
        E("ACT", lambda h: h.activation(out=t["osums"][:, :], in_=t["osums"][:, :],
                                        func=AF.Ln), reads=["osums"], writes=["osums"])
        E("ACT", lambda h: h.activation(out=t["oscale"][:, :], in_=t["osums"][:, :],
                                        func=AF.Exp, scale=-0.5),
          reads=["osums"], writes=["oscale"])
        E("POOL", lambda h: h.tensor_mul(out=t["oscr"][:, :], in0=t["otok"][:, :],
                                         in1=self.bc8("oscale")),
          reads=["otok", "oscale"], writes=["oscr"])
        def otr(h):
            for hh in range(8):
                ins = h.transpose(p["pm1"][:, hh * 128:(hh + 1) * 128],
                                  t["oscr"][:, hh * 128:(hh + 1) * 128],
                                  t["identf"][:, :])
            return ins
        E("PE", otr, reads=["oscr", "identf"], writes=PM1)
        o2f_ap = bass.AP(t["o2f"][:, :].tensor, c * 128,
                         [[4096, 128], [512, 8], [1, 128]])
        gsl_ap = bass.AP(t["gsl"][:, :].tensor, c * 128,
                         [[4096, 128], [512, 8], [1, 128]])
        E("DVE", lambda h: h.tensor_mul(out=o2f_ap, in0=p["pm1"][:, :],
                                        in1=gsl_ap),
          reads=PM1 + ["gsl"], writes=["o2f"])
        if dbg == "Omat":
            dump("otok", ["otok"])
            return
        if amx == "o":
            return
        def ktr(h):
            for hh in range(8):
                ins = h.transpose(p["ptb"][:, hh * 128:(hh + 1) * 128],
                                  ch("ksl", hh), t["identb"][:, :])
            return ins
        E("PE", ktr, reads=["ksl", "identb"], writes=["ptb"])
        E("DVE", lambda h: h.tensor_mul(out=t["Kts"][:, :], in0=p["ptb"][:, :],
                                        in1=self.bc8("dcy")),
          reads=["ptb", "dcy"], writes=["Kts"])
        def smm(h):
            for hh in range(8):
                ins = h.matmul(p["pm0"][:, hh * 128:(hh + 1) * 128],
                               t["Kts"][:, hh * 128:(hh + 1) * 128],
                               t["Xb"][:, hh * 128:(hh + 1) * 128],
                               start=True, stop=True)
            return ins
        E("PE", smm, reads=["Kts", "Xb"], writes=PM0)
        E("POOL", lambda h: h.tensor_mul(out=t["S0"][:, :], in0=t["S0"][:, :],
                                         in1=self.bc8("gCb")),
          reads=["S0", "gCb"], writes=["S0"])
        E("DVE", lambda h: h.tensor_add(out=t["S0"][:, :], in0=t["S0"][:, :],
                                        in1=p["pm0"][:, :]),
          reads=["S0"] + PM0, writes=["S0"])
        E("ACT", lambda h: h.activation(out=t["S0b"][:, :], in_=t["S0"][:, :],
                                        func=AF.Copy),
          reads=["S0"], writes=["S0b"])

    def _post_stage(self, seq, sub):
        t, d, p, E = self.t, self.d, self.p, self.E
        self.enter_phase("post")
        for half in range(2):
            l0 = half * 256
            oww, _, owr = self._win(full=True)
            self.dma("SPD", oww[:, :], d["s_ow"][:, :], writes=owr)
            self.bcast_row(t["r_mu"][0:1, l0:l0 + 256], "pb1", width=256,
                           reads=["r_mu"])
            self.bcast_row(t["r_inv"][0:1, l0:l0 + 256], "pb2", width=256,
                           reads=["r_inv"])
            for dc in range(8):
                if dc % 2 == 0:
                    pb, pbr = p["pm0"], ["pm0:0", "pm0:1"]
                    f0, f0r = t["fscr0"][:, 0:256], "fscr0"
                    f1, f1r = t["fscr1"][:, 0:256], "fscr1"
                else:
                    pb, pbr = p["pm1"], ["pm1:0", "pm1:1"]
                    f0, f0r = t["Esl"][:, 0:256], "Esl"
                    f1, f1r = t["Esl"][:, 256:512], "Esl"
                def grpo(h, oww=oww, dc=dc, l0=l0, pb=pb):
                    for di in range(8):
                        ins = h.matmul(
                            pb[:, 0:256],
                            oww[:, di * 1024 + dc * 128:di * 1024 + (dc + 1) * 128],
                            t["o2f"][:, di * 512 + l0:di * 512 + l0 + 256],
                            start=(di == 0), stop=(di == 7))
                    return ins
                E("PE", grpo, reads=owr + ["o2f"], writes=pbr)
                E("ACT", lambda h, pb=pb, f0=f0: h.activation(
                    out=f0, in_=pb[:, 0:256], func=AF.Copy, scale=0.5),
                  reads=pbr, writes=[f0r])
                xs = t["x_sub"][:, dc * 512 + l0:dc * 512 + l0 + 256]
                E("DVE", lambda h, xs=xs, f1=f1: h.tensor_sub(
                    out=f1, in0=xs, in1=p["pb1"][:, 0:256]),
                  reads=["x_sub", "pb1"], writes=[f1r])
                E("DVE", lambda h, f1=f1: h.tensor_mul(
                    out=f1, in0=f1, in1=p["pb2"][:, 0:256]),
                  reads=[f1r, "pb2"], writes=[f1r])
                E("DVE", lambda h, dc=dc, f0=f0, f1=f1: h.tensor_add(
                    out=t["a1sl"][:, dc * 256:(dc + 1) * 256],
                    in0=f1, in1=f0),
                  reads=[f1r, f0r], writes=["a1sl"])
                E("DVE", lambda h, xs=xs, dc=dc: h.tensor_add(
                    out=xs, in0=xs, in1=t["a1sl"][:, dc * 256:(dc + 1) * 256]),
                  reads=["x_sub", "a1sl"], writes=["x_sub"])
            E("ACT", lambda h: h.activation(out=t["xbfs"][:, 0:2048],
                                            in_=t["a1sl"][:, :], func=AF.Copy),
              reads=["a1sl"], writes=["xbfs"])
            E("ACT", lambda h: h.activation(out=t["xbfs"][:, 0:2048],
                                            in_=t["xbfs"][:, 0:2048], func=AF.Square),
              reads=["xbfs"], writes=["xbfs"])
            def h2sum(h):
                for dc in range(8):
                    ins = h.matmul(p["pb0"][0:1, 0:256], t["ones_colb"][:, :],
                                   t["xbfs"][:, dc * 256:(dc + 1) * 256],
                                   start=(dc == 0), stop=(dc == 7))
                return ins
            E("PE", h2sum, reads=["xbfs", "ones_colb"], writes=["pb0"])
            E("DVE", lambda h: h.tensor_scalar(
                out=t["r_t1"][0:1, 0:256], in0=p["pb0"][0:1, 0:256],
                scalar1=float(1.0 / 1024), scalar2=float(1e-6),
                op0=AL.mult, op1=AL.add), reads=["pb0"], writes=["r_t1"])
            E("ACT", lambda h: h.activation(out=t["r_t2"][0:1, 0:256],
                                            in_=t["r_t1"][0:1, 0:256], func=AF.Ln),
              reads=["r_t1"], writes=["r_t2"])
            E("ACT", lambda h: h.activation(out=t["r_t2"][0:1, 0:256],
                                            in_=t["r_t2"][0:1, 0:256],
                                            func=AF.Exp, scale=-0.5),
              reads=["r_t2"], writes=["r_t2"])
            self.bcast_row(t["r_t2"][0:1, 0:256], "pb0", width=256,
                           reads=["r_t2"])
            for dc in range(8):
                if dc % 2 == 0:
                    f1, f1r = t["fscr1"][:, 0:256], "fscr1"
                else:
                    f1, f1r = t["Esl"][:, 512:768], "Esl"
                E("DVE", lambda h, dc=dc, f1=f1: h.tensor_mul(
                    out=f1,
                    in0=t["a1sl"][:, dc * 256:(dc + 1) * 256],
                    in1=p["pb0"][:, 0:256]), reads=["a1sl", "pb0"], writes=[f1r])
                E("ACT", lambda h, dc=dc, f1=f1: h.activation(
                    out=t["h2sl"][:, dc * 256:(dc + 1) * 256],
                    in_=f1, func=AF.Identity,
                    scale=t["ln2c"][:, dc:dc + 1]),
                  reads=[f1r, "ln2c"], writes=["h2sl"])
            for hq in range(4):
                win, off, wr = self._win(full=True)
                in_ap = bass.AP(d["s_fcw"].tensor, hq * 1024,
                                [[32768, 128], [4096, 8], [1, 1024]])
                self.dma("SPD", win[:, :].rearrange("p (a j) -> p a j", j=1024),
                         in_ap, writes=wr)
                for hcl in range(8):
                    hc = hq * 8 + hcl
                    pb2_ = p[f"pb{1 + hc % 2}"]
                    pbr2 = f"pb{1 + hc % 2}"
                    def grpf(h, win=win, hcl=hcl, pb2_=pb2_):
                        for dc in range(8):
                            ins = h.matmul(
                                pb2_[:, 0:256],
                                win[:, dc * 1024 + hcl * 128:dc * 1024 + (hcl + 1) * 128],
                                t["h2sl"][:, dc * 256:(dc + 1) * 256],
                                start=(dc == 0), stop=(dc == 7))
                        return ins
                    E("PE", grpf, reads=wr + ["h2sl"], writes=[pbr2])
                    E("ACT", lambda h, pb2_=pb2_, hc=hc: h.activation(
                        out=t["fcsl"][:, hc * 256:(hc + 1) * 256],
                        in_=pb2_[:, 0:256], func=AF.Silu),
                      reads=[pbr2], writes=["fcsl"])
            for dc in range(8):
                win, off, wr = self._win(full=False)
                self.dma("SPD", win[:, off:off + 4096],
                         d["s_pw"][:, dc * 4096:(dc + 1) * 4096], writes=wr)
                pb2_ = p[f"pb{1 + dc % 2}"]
                pbr2 = f"pb{1 + dc % 2}"
                def grpp(h, win=win, off=off, pb2_=pb2_):
                    for hc in range(32):
                        ins = h.matmul(pb2_[:, 0:256],
                                       win[:, off + hc * 128:off + (hc + 1) * 128],
                                       t["fcsl"][:, hc * 256:(hc + 1) * 256],
                                       start=(hc == 0), stop=(hc == 31))
                    return ins
                E("PE", grpp, reads=wr + ["fcsl"], writes=[pbr2])
                if dc % 2 == 0:
                    f0, f0r = t["fscr0"][:, 0:256], "fscr0"
                else:
                    f0, f0r = t["Esl"][:, 768:1024], "Esl"
                E("ACT", lambda h, pb2_=pb2_, f0=f0: h.activation(
                    out=f0, in_=pb2_[:, 0:256], func=AF.Copy,
                    scale=0.5), reads=[pbr2], writes=[f0r])
                xs = t["x_sub"][:, dc * 512 + l0:dc * 512 + l0 + 256]
                E("DVE", lambda h, xs=xs, f0=f0: h.tensor_add(
                    out=xs, in0=xs, in1=f0),
                  reads=["x_sub", f0r], writes=["x_sub"])

    def build(self):
        import os
        stages = os.environ.get("ANT_STAGES", "full")
        self.prep()
        for seq in range(SEQ_PER_CORE):
            self.load_x_tile(seq)
            if stages in ("ffn1", "ffn1attn", "full"):
                self.ffn_half(seq)
            if stages in ("ffn1attn", "full"):
                self.attn(seq)
            if stages == "full":
                self.ffn_half(seq)
            self.store_x_tile(seq)
        self.em.final_wait_all("POOL")


def build_program():
    _register_ops()
    nc = bass.Bass("TRN2", target_bir_lowering=False)
    es = ExitStack()
    kb = KB(nc, es)
    with es, nc.Block() as block:
        kb.build()
        kb.em.install(block)
    lower_extended_insts(nc)
    return nc, kb



_PROG_CACHE = {}


def kernel(L0_attn_x, L0_attn_ln1_w, L0_attn_qw, L0_attn_kw, L0_attn_vw,
           L0_attn_bw, L0_attn_aw, L0_attn_A_log, L0_attn_dt_bias, L0_attn_gw,
           L0_attn_onorm_w, L0_attn_ow, L0_attn_ln2_w, L0_attn_fcw, L0_attn_pw,
           L0_ffn_w1, L0_ffn_b1, L0_ffn_w2, L0_ffn_b2):
    consts = _get_host_consts()
    if "nc" not in _PROG_CACHE:
        nc, kb = build_program()
        _PROG_CACHE["nc"] = nc
    nc = _PROG_CACHE["nc"]

    f32 = np.float32
    asf = lambda a: np.ascontiguousarray(a, dtype=f32)
    shared = {
        "qw": asf(L0_attn_qw), "kw": asf(L0_attn_kw), "vw": asf(L0_attn_vw),
        "gw": asf(L0_attn_gw), "ow": asf(L0_attn_ow), "fcw": asf(L0_attn_fcw),
        "pw": asf(L0_attn_pw), "w1": asf(L0_ffn_w1), "w2": asf(L0_ffn_w2),
        "awbw": asf(np.concatenate([np.asarray(L0_attn_aw),
                                    np.asarray(L0_attn_bw)], axis=1)),
        "b1r": asf(np.asarray(L0_ffn_b1).reshape(32, 128).T),
        "b2r": asf(np.asarray(L0_ffn_b2).reshape(8, 128).T),
        "ln1c": asf(np.asarray(L0_attn_ln1_w).reshape(8, 128).T),
        "ln2c": asf(np.asarray(L0_attn_ln2_w).reshape(8, 128).T),
        "onormc": asf(np.asarray(L0_attn_onorm_w).reshape(128, 1)),
        "alog": asf(np.asarray(L0_attn_A_log).reshape(8, 1)),
        "dtb": asf(np.asarray(L0_attn_dt_bias).reshape(8, 1)),
        "identb": consts["identb"], "identf": consts["identf"],
        "sel8": consts["sel8"], "e8": consts["e8"],
        "trineg": consts["trineg"], "trinc": consts["trinc"],
        "identr": consts["identr"],
    }
    x = asf(L0_attn_x)
    in_maps = []
    for core in range(NCORE):
        im = dict(shared)
        im["x"] = np.ascontiguousarray(x[core * 2:(core + 1) * 2])
        im["m32f"] = np.ascontiguousarray(
            _feature_major_mask(consts["m32"][core * 2:(core + 1) * 2])
        ).astype(ml_dtypes.bfloat16)
        im["mdf"] = np.ascontiguousarray(
            _feature_major_mask(consts["md"][core * 2:(core + 1) * 2])
        ).astype(ml_dtypes.bfloat16)
        in_maps.append(im)
    res = run_bass_kernel_spmd(nc, in_maps, core_ids=list(range(NCORE)))
    out = np.concatenate([np.asarray(r["y"], dtype=np.float32)
                          for r in res.results], axis=0)
    return out.reshape(16, T, D)


# version 11 (final)
# speedup vs baseline: 1.0277x; 1.0277x over previous
import numpy as np
import ml_dtypes
from contextlib import ExitStack

import concourse.bass as bass
import concourse.mybir as mybir
from concourse.bass_utils import run_bass_kernel_spmd
from concourse.library_overlay import lower_extended_insts

F32 = mybir.dt.float32
F16 = mybir.dt.float16
BF16 = mybir.dt.bfloat16
F8 = mybir.dt.float8e4
U8 = mybir.dt.uint8
AL = mybir.AluOpType
AF = mybir.ActivationFunctionType

D = 1024
H = 8
HD = 128
T = 2048
NCORE = 8
SEQ_PER_CORE = 2
SUB = 512
NSUB = T // SUB
CH = 128
NCH_SUB = SUB // CH


_OPS = {}


def _f(x):
    return np.float64 if np.asarray(x).dtype == np.float64 else np.float32


def _col(v, f):
    v = np.asarray(v, f)
    return v.reshape(-1, 1) if v.ndim else v


def _ref_qpos(in0, in1, s0, s1, imm2):
    f = _f(in0)
    z = np.asarray(in0, f)
    c0 = _col(s0, f)
    c1 = _col(s1, f)
    a = (z * c0).astype(f)
    b = (a - z).astype(f)
    v = (a - b).astype(f)
    t = (z + c1).astype(f)
    s = (t - c1).astype(f)
    return np.where(z >= f(imm2), v, s)


def _ref_qsel(in0, in1, s0, s1, imm2):
    f = _f(in0)
    z = np.asarray(in0, f)
    c1 = _col(s1, f)
    t = (z + c1).astype(f)
    s = (t - c1).astype(f)
    q = (z * z).astype(f)
    return np.where(q >= f(imm2), np.asarray(in1, f), s)


def _ref_lerp(in0, in1, s0, s1, imm2):
    f = _f(in0)
    a = np.asarray(in0, f)
    d = (np.asarray(in1, f) - a).astype(f)
    m = (d * _col(s0, f)).astype(f)
    return (a + m).astype(f)


def _ref_nlerp(in0, in1, s0, s1, imm2):
    f = _f(in0)
    t = (np.asarray(in1, f) * _col(s0, f)).astype(f)
    u = (f(1.0) - t).astype(f)
    return (np.asarray(in0, f) * u).astype(f)


def _ref_td4s(in0, in1, s0, s1, imm2):
    f = _f(in0)
    s = (np.asarray(in0, f) ** 2).astype(f)
    a = (s >= _col(s0, f)).astype(f)
    b = (s >= _col(s1, f)).astype(f)
    c = (s >= f(imm2)).astype(f)
    return ((a - b) - b) + c


def _ref_td4w(in0, in1, s0, s1, imm2):
    f = _f(in0)
    s = (np.asarray(in0, f) ** 2).astype(f)
    a = (s >= _col(s0, f)).astype(f)
    b = (s >= _col(s1, f)).astype(f)
    return a - b


def _register_ops():
    if _OPS:
        return _OPS
    from concourse.dve_op import DveOp
    from concourse.dve_spec import Spec, Src0, Src1, C0, C1, C2, One, select, sq, lower
    from concourse import dve_ops as dops

    a = Src0 * C0
    b = a - Src0
    v = a - b
    t = Src0 + C1
    s = t - C1
    spec_qpos = Spec(body=select(Src0 >= C2, v, s))
    spec_qsel = Spec(body=select(sq(Src0) >= C2, Src1, s))
    dd = Src1 - Src0
    spec_lerp = Spec(body=Src0 + dd * C0)
    tt = Src1 * C0
    spec_nlerp = Spec(body=Src0 * (One - tt))
    sx = sq(Src0)
    ca = (sx >= C0)
    cb = (sx >= C1)
    cc = (sx >= C2)
    spec_td4s = Spec(body=((ca - cb) - cb) + cc)
    spec_td4w = Spec(body=ca - cb)

    defs = [
        ("ANT_QPOS", spec_qpos, _ref_qpos),
        ("ANT_QSEL", spec_qsel, _ref_qsel),
        ("ANT_LERP", spec_lerp, _ref_lerp),
        ("ANT_NLERP", spec_nlerp, _ref_nlerp),
        ("ANT_TD4S", spec_td4s, _ref_td4s),
        ("ANT_TD4W", spec_td4w, _ref_td4w),
    ]
    replaceable = ["DVEHASH32L_SEED", "DVEHASH32L_BODY", "DVEHASH32L_DRAIN",
                   "BITWISE_XOR_TT_BV", "SHIFT_RIGHT_TT_BV",
                   "CODY_WAITE_CASCADE", "ADD_RANGE_WRAP"]
    for name, spec, ref in defs:
        if name in dops._SUB_OPCODE_FOR_NAME:
            _OPS[name] = next(o for o in dops.OPS if o.name == name)
            continue
        sha = {ver: lower(spec, ver=ver, perf_en=False, name=name).sha(ver)
               for ver in ("v3", "v4")}
        op = DveOp(name, spec, subdim=False, uops_sha=sha, reference=ref)
        if max(dops._SUB_OPCODE_FOR_NAME.values()) < 0x1F:
            dops.OPS.append(op)
            row = dops._CUSTOM_DVE_ROW_BASE + len(dops.OPS) - 1
        else:
            victim = replaceable.pop()
            i = next(i for i, o in enumerate(dops.OPS) if o.name == victim)
            row = dops._SUB_OPCODE_FOR_NAME.pop(victim)
            dops.OPS[i] = op
            dops.CUSTOM_DVE_SPECS.pop(victim, None)
            dops.CUSTOM_DVE_REFS.pop(victim, None)
        dops._SUB_OPCODE_FOR_NAME[name] = row
        dops.CUSTOM_DVE_SPECS[name] = spec
        dops.CUSTOM_DVE_REFS[name] = ref
        _OPS[name] = op
    return _OPS



COMPUTE = ("PE", "DVE", "ACT", "POOL", "SP")
DMAQ = {"SPD": ("SP", 12), "ACTD": ("ACT", 6), "POOLD": ("POOL", 12)}


class Emitter:
    def __init__(self, nc, es):
        self.nc = nc
        self.streams = list(COMPUTE)
        self.exec_eng = {e: e for e in COMPUTE}
        self.dma_rot = {}
        for q, (eng, nslots) in DMAQ.items():
            slots = [f"{q}{i}" for i in range(nslots)]
            self.dma_rot[q] = (nslots, slots)
            for s in slots:
                self.streams.append(s)
                self.exec_eng[s] = eng
        self.sems = {n: es.enter_context(nc.semaphore("sem_" + n))
                     for n in self.streams}
        self.count = dict.fromkeys(self.streams, 0)
        self.rotidx = dict.fromkeys(DMAQ, 0)
        self.clock = {e: dict.fromkeys(self.streams, 0) for e in COMPUTE}
        self.snap = {s: {0: dict.fromkeys(self.streams, 0)} for s in self.streams}
        self.regions = {}
        self.prog = {e: [] for e in COMPUTE}
        self.n_waits = 0
        self.n_ins = 0

    def _region(self, name):
        return self.regions.setdefault(name, {"w": None, "r": {}})

    def _wait(self, eng, stream, cnt):
        if cnt <= 0 or self.clock[eng].get(stream, 0) >= cnt:
            return
        self.prog[eng].append(("wait", stream, cnt))
        self.n_waits += 1
        snap = self.snap[stream].get(cnt)
        if snap is None:
            keys = [k for k in self.snap[stream] if k <= cnt]
            snap = self.snap[stream][max(keys)] if keys else {}
        c = self.clock[eng]
        c[stream] = max(c.get(stream, 0), cnt)
        for s, v in snap.items():
            if c.get(s, 0) < v:
                c[s] = v

    def emit(self, stream, fn, reads=(), writes=(), tag=None):
        if stream in DMAQ:
            nslots, slots = self.dma_rot[stream]
            slot = slots[self.rotidx[stream] % nslots]
            self.rotidx[stream] += 1
            stream = slot
            inc = 16
        else:
            inc = 1
        eng = self.exec_eng[stream]
        for rn in reads:
            r = self._region(rn)
            if r["w"] is not None:
                self._wait(eng, *r["w"])
        for wn in writes:
            r = self._region(wn)
            if r["w"] is not None:
                self._wait(eng, *r["w"])
            for rs, rc in r["r"].items():
                self._wait(eng, rs, rc)
        self.count[stream] += inc
        cnt = self.count[stream]
        self.prog[eng].append(("ins", fn, stream, inc, tag))
        self.n_ins += 1
        snapshot = dict(self.clock[eng])
        snapshot[stream] = cnt
        self.snap[stream][cnt] = snapshot
        for rn in reads:
            self._region(rn)["r"][stream] = cnt
        for wn in writes:
            r = self._region(wn)
            r["w"] = (stream, cnt)
            r["r"] = {}
        return cnt

    def merge_regions(self, srcs, dst):
        d = self._region(dst)
        for s in srcs:
            if s not in self.regions:
                continue
            r = self.regions[s]
            if r["w"] is not None:
                ws, wc = r["w"]
                d["r"][ws] = max(d["r"].get(ws, 0), wc)
            for rs, rc in r["r"].items():
                d["r"][rs] = max(d["r"].get(rs, 0), rc)

    def final_wait_all(self, eng="POOL"):
        for s in self.streams:
            if s != eng:
                self._wait(eng, s, self.count[s])

    def replay(self, eng, h):
        for item in self.prog[eng]:
            if item[0] == "wait":
                _, stream, cnt = item
                h.wait_ge(self.sems[stream], cnt)
            else:
                _, fn, stream, inc, tag = item
                ins = fn(h)
                ins.then_inc(self.sems[stream], inc)

    def install(self, block):
        @block.tensor
        def _(h):
            self.replay("PE", h)

        @block.vector
        def _(h):
            self.replay("DVE", h)

        @block.scalar
        def _(h):
            self.replay("ACT", h)

        @block.gpsimd
        def _(h):
            self.replay("POOL", h)

        @block.sync
        def _(h):
            self.replay("SP", h)



def _threefry2x32_pair(k1, k2, x0, x1):
    def rotl(x, d):
        return ((x << np.uint32(d)) | (x >> np.uint32(32 - d))).astype(np.uint32)
    rot = [[13, 15, 26, 6], [17, 29, 16, 24]]
    x0 = x0.astype(np.uint32).copy()
    x1 = x1.astype(np.uint32).copy()
    ks0, ks1 = np.uint32(k1), np.uint32(k2)
    ks2 = np.uint32(ks0 ^ ks1 ^ np.uint32(0x1BD11BDA))
    x0 = (x0 + ks0).astype(np.uint32)
    x1 = (x1 + ks1).astype(np.uint32)
    ks = [ks1, ks2, ks0, ks1, ks2, ks0]
    for i in range(5):
        for r in rot[i % 2]:
            x0 = (x0 + x1).astype(np.uint32)
            x1 = rotl(x1, r) ^ x0
        x0 = (x0 + ks[i]).astype(np.uint32)
        x1 = (x1 + np.uint32(ks[i + 1]) + np.uint32(i + 1)).astype(np.uint32)
    return x0, x1


def _dropout_masks():
    n = 16 * T * D
    x0 = np.zeros(n, np.uint32)
    x1 = np.arange(n, dtype=np.uint32)
    b0, b1 = _threefry2x32_pair(0, 42, x0, x1)
    bits32 = b0 ^ b1
    u32 = ((bits32 >> np.uint32(9)) | np.uint32(0x3F800000)).view(np.float32) - np.float32(1.0)
    keep32 = u32 < np.float32(0.8)
    bits64 = (b0.astype(np.uint64) << np.uint64(32)) | b1.astype(np.uint64)
    u64 = ((bits64 >> np.uint64(12)) | np.uint64(0x3FF0000000000000)).view(np.float64) - 1.0
    keep64 = u64 < np.float64(0.8)
    m32 = np.where(keep32, np.float32(0.625), np.float32(0.0))
    m64 = np.where(keep64, np.float32(0.625), np.float32(0.0))
    md = m64 - m32
    return (m32.reshape(16, T, D), md.reshape(16, T, D))


def _feature_major_mask(m):
    return np.ascontiguousarray(
        m.reshape(SEQ_PER_CORE, T, H, HD).transpose(0, 2, 3, 1))


def _host_consts():
    c = {}
    c["identb"] = np.eye(128).astype(ml_dtypes.bfloat16)
    c["identf"] = np.eye(128, dtype=np.float32)
    sel8 = np.zeros((8, 8 * 128), np.float32)
    for h in range(8):
        sel8[h, h * 128:(h + 1) * 128] = 1.0
    c["sel8"] = sel8
    e8 = np.zeros((128, 8 * 8), np.float32)
    for h in range(8):
        e8[:, h * 8 + h] = 1.0
    c["e8"] = e8.astype(ml_dtypes.bfloat16)
    iu = np.triu(np.ones((128, 128), np.float32), 1)
    c["trineg"] = np.tile(-iu, (1, 8)).astype(np.float32)
    inc = np.triu(np.ones((128, 128), np.float32), 0)
    c["trinc"] = np.tile(inc, (1, 8)).astype(np.float32)
    c["identr"] = np.tile(np.eye(128), (1, 8)).astype(ml_dtypes.bfloat16)
    return c


_CONST_CACHE = {}


def _get_host_consts():
    if not _CONST_CACHE:
        _CONST_CACHE.update(_host_consts())
        m32, md = _dropout_masks()
        _CONST_CACHE["m32"] = m32
        _CONST_CACHE["md"] = md
    return _CONST_CACHE


MSUB16 = float(1.5 * 2.0**28)
MSUB43 = float(1.5 * 2.0**43)
NEGBIG = float(-3.0e38)


class KB:
    def __init__(self, nc, es):
        self.nc = nc
        self.es = es
        self.em = Emitter(nc, es)
        self.ops = _register_ops()
        self._mk_dram()
        self._mk_sbuf()
        self._mk_psum()

    def _dram_in(self, name, shape, dt=F32):
        return self.nc.dram_tensor(name, list(shape), dt, kind="ExternalInput").ap()

    def _mk_dram(self):
        d = self.d = {}
        d["x"] = self._dram_in("x", (SEQ_PER_CORE, T, D))
        for w, sh in [("qw", (D, D)), ("kw", (D, D)), ("vw", (D, D)),
                      ("gw", (D, D)), ("ow", (D, D)), ("fcw", (D, 4 * D)),
                      ("pw", (4 * D, D)), ("w1", (4 * D, D)), ("w2", (D, 4 * D))]:
            d[w] = self._dram_in(w, sh)
        d["awbw"] = self._dram_in("awbw", (D, 16))
        d["b1r"] = self._dram_in("b1r", (128, 32))
        d["b2r"] = self._dram_in("b2r", (128, 8))
        d["ln1c"] = self._dram_in("ln1c", (128, 8))
        d["ln2c"] = self._dram_in("ln2c", (128, 8))
        d["onormc"] = self._dram_in("onormc", (128, 1))
        d["alog"] = self._dram_in("alog", (8, 1))
        d["dtb"] = self._dram_in("dtb", (8, 1))
        d["identb"] = self._dram_in("identb", (128, 128), BF16)
        d["identf"] = self._dram_in("identf", (128, 128))
        d["sel8"] = self._dram_in("sel8", (8, 8 * 128))
        d["e8"] = self._dram_in("e8", (128, 8 * 8), BF16)
        d["trineg"] = self._dram_in("trineg", (128, 8 * 128))
        d["trinc"] = self._dram_in("trinc", (128, 8 * 128))
        d["identr"] = self._dram_in("identr", (128, 8 * 128), BF16)
        d["m32f"] = self._dram_in("m32f", (SEQ_PER_CORE, 8, 128, T), BF16)
        d["mdf"] = self._dram_in("mdf", (SEQ_PER_CORE, 8, 128, T), BF16)
        d["y"] = self.nc.dram_tensor("y", [SEQ_PER_CORE, T, D], F32,
                                     kind="ExternalOutput").ap()
        sc = self.nc.dram_tensor
        d["s_x"] = sc("s_x", [D, T], F32).ap()
        for w, cols in [("s_qw", 8 * D), ("s_kw", 8 * D), ("s_vw", 8 * D),
                        ("s_gw", 8 * D), ("s_ow", 8 * D),
                        ("s_fcw", 8 * 4 * D), ("s_pw", 32 * D)]:
            d[w] = sc(w, [128, cols], BF16).ap()
        d["s_w1t"] = sc("s_w1t", [128, 8 * 4 * D], F8).ap()
        d["s_w2t"] = sc("s_w2t", [128, 32 * D], F8).ap()

    def _mk_sbuf(self):
        nc, es = self.nc, self.es
        t = self.t = {}

        def sb(name, shape, dt=F32):
            t[name] = es.enter_context(nc.sbuf_tensor("sb_" + name, list(shape), dt))
            return t[name]

        sb("identb", (128, 128), BF16)
        sb("identf", (128, 128))
        sb("sel8", (8, 8 * 128))
        sb("e8", (128, 8 * 8), BF16)
        sb("trineg", (128, 8 * 128))
        sb("trinc", (128, 8 * 128))
        sb("identr", (128, 8 * 128), BF16)
        sb("ones_row", (1, 128))
        sb("ones_colb", (128, 1), BF16)
        sb("ln1c", (128, 8))
        sb("ln2c", (128, 8))
        sb("onormc", (128, 1))
        sb("alog", (8, 1))
        sb("dtb", (8, 1))
        sb("nA", (8, 1))
        sb("awbw", (128, 8 * 16), BF16)
        for n in ("ind", "cv16", "cv4", "epsp", "epsn"):
            sb(n, (128, 1))
        sb("b1q", (128, 32))
        sb("b2q", (128, 8))
        sb("b1_8", (128, 32), F8)
        sb("b2_8", (128, 8), F8)
        sb("S0", (128, 8 * 128))
        sb("S0b", (128, 8 * 128), BF16)
        for n in ("r_mu", "r_inv", "r_s", "r_t1", "r_t2"):
            sb(n, (1, 512))
        for n in ("qn", "kn", "r_za", "r_zb", "r_sca", "r_scb",
                  "r_lb", "r_beta", "r_g", "r_glb"):
            sb(n, (8, 512))
        sb("gtile", (8, T))
        sb("gprev", (8, 1))
        sb("gz", (8, 1))
        for n in ("A0", "Qs", "A1", "Td"):
            sb(n, (128, 512), F16)
        sb("fscr0", (128, 1024))
        sb("fscr1", (128, 1024))
        sb("xbfs", (128, 4096), BF16)
        sb("x_sub", (128, 8 * 512))
        for n in ("mk0", "mk1", "mk2", "mk3"):
            sb(n, (128, 512), BF16)
        sb("wwin0", (128, 8192), BF16)
        sb("wwin1", (128, 8192), BF16)
        sb("gcolT", (128, 32))
        sb("gamc", (128, 8))
        sb("dcy", (128, 8))
        sb("ndcy", (128, 8))
        sb("gCb", (128, 8))
        sb("osums", (128, 8))
        sb("oscale", (128, 8))
        sb("qnT", (128, 32))
        base = 229344
        layout = [
            ("qsl", 0, 8192, BF16), ("ksl", 8192, 8192, BF16),
            ("vsl", 16384, 8192, BF16), ("gsl", 24576, 8192, BF16),
            ("hsl", 32768, 8192, BF16),
            ("fcsl", 0, 16384, BF16), ("a1sl", 16384, 8192, F32),
            ("h2sl", 24576, 4096, BF16),
            ("h8sl", 0, 16384, F8), ("u8sl", 16384, 4096, F8),
            ("f8w0", 20480, 8192, F8), ("f8w1", 28672, 8192, F8),
            ("o2f", 40960, 8192, BF16),
            ("A0b", 40960, 1024, F16), ("Qsb", 41984, 1024, F16),
            ("A1b", 43008, 1024, F16), ("Tdb", 44032, 1024, F16),
            ("fscr2", 45056, 2048, F32), ("fscr3", 47104, 2048, F32),
            ("Esl", 49152, 4096, F32), ("otok", 53248, 4096, F32),
            ("oscr", 57344, 4096, F32),
            ("Xa", 61440, 2048, BF16), ("XTa", 63488, 2048, BF16),
            ("Xb", 65536, 2048, BF16), ("XTb", 67584, 2048, BF16),
            ("Zb", 69632, 2048, BF16), ("Ab", 71680, 2048, BF16),
            ("Kts", 73728, 2048, BF16),
            ("pzA", 0, 4096, F32), ("pzB", 4096, 4096, F32),
            ("pscr0B", 8192, 4096, F32), ("pscr1B", 12288, 4096, F32),
            ("pwsB", 20480, 16384, F32),
        ]
        arena_bytes = 75776
        arena_base = base - arena_bytes
        self.arena_groups = {
            "ffn": ["h8sl", "u8sl", "f8w0", "f8w1", "A0b", "Qsb", "A1b",
                    "Tdb", "fscr2", "fscr3"],
            "attn_gen": ["qsl", "ksl", "vsl", "gsl", "hsl"],
            "post": ["fcsl", "a1sl", "h2sl"],
        }
        for name, off, nbytes, dt in layout:
            isz = {F32: 4, BF16: 2, F8: 1, F16: 2}[dt]
            t[name] = nc.alloc_sbuf_tensor_at(
                name, [128, nbytes // isz], dt, offset=arena_base + off)
        left = nc._sbuf_addr_for_side("left")
        assert left <= arena_base, (
            f"SBUF overflow: left ptr {left} > arena base {arena_base}")
        self.sbuf_slack = arena_base - left

    def _mk_psum(self):
        nc, es = self.nc, self.es
        p = self.p = {}
        for n in ("pb0", "pb1", "pb2"):
            p[n] = es.enter_context(nc.psum_tensor(n, [128, 512], F32))
        for n in ("pm0", "pm1"):
            p[n] = es.enter_context(nc.psum_tensor(n, [128, 1024], F32))
        p["ptb"] = es.enter_context(nc.psum_tensor("ptb", [128, 1024], BF16))

    def E(self, *a, **k):
        return self.em.emit(*a, **k)

    def cust(self, opname, out, in0, in1=None, s0=0.0, s1=0.0, imm2=0.0,
             reads=(), writes=()):
        op = self.ops[opname]
        kw = dict(out=out, in0=in0, s0=s0, s1=s1, imm2=imm2)
        if in1 is not None:
            kw["in1"] = in1
        self.E("DVE", lambda h: h._custom_dve(op, **kw),
               reads=reads, writes=writes)

    def sgn_q43(self, zin, zr, wout, wr, adj_w=False, scr=("fscr0", "fscr1")):
        t = self.t
        n = zin.shape[-1]
        n0, n1 = scr
        s0 = t[n0][:, 0:n]
        s1 = t[n1][:, 0:n]
        src, srcr = zin, zr
        if adj_w:
            self.cust("ANT_TD4W", s0, zin, s0=float(2.0**-10), s1=float(2.0**-4),
                      imm2=1.0, reads=[zr], writes=[n0])
            self.cust("ANT_NLERP", s1, zin, in1=s0, s0=t["epsp"][:, :],
                      reads=[zr, n0, "epsp"], writes=[n1])
            src, srcr = s1, n1
        self.cust("ANT_QPOS", s0, src, s0=t["cv4"][:, :], s1=MSUB43,
                  imm2=NEGBIG, reads=[srcr, "cv4"], writes=[n0])
        self.cust("ANT_QSEL", s0, src, in1=s0, s1=MSUB43, imm2=float(2.0**-12),
                  reads=[srcr, n0], writes=[n0])
        self.cust("ANT_LERP", wout, src, in1=s0, s0=t["ind"][:, :],
                  reads=[srcr, n0, "ind"], writes=[wr])

    def dma(self, q, out, in_, reads=(), writes=()):
        self.E(q, lambda h: h.dma_start(out=out, in_=in_),
               reads=reads, writes=writes)

    def enter_phase(self, grp):
        others = [n for g, names in self.arena_groups.items() if g != grp
                  for n in names]
        for name in self.arena_groups[grp]:
            self.em.merge_regions(others, name)

    def prep(self):
        t, d, p, E = self.t, self.d, self.p, self.E
        for n in ("identb", "identf", "sel8", "e8", "trineg", "trinc",
                  "identr", "ln1c", "ln2c", "onormc", "alog", "dtb"):
            self.dma("SPD", t[n][:, :], d[n][:, :], writes=[n])
        self.dma("SPD", t["fscr0"][:, 0:128].rearrange("p (a w) -> p a w", w=16),
                 bass.AP(d["awbw"].tensor, 0, [[16, 128], [128 * 16, 8], [1, 16]]),
                 writes=["fscr0"])
        E("DVE", lambda h: h.tensor_copy(out=t["awbw"][:, :], in_=t["fscr0"][:, 0:128]),
          reads=["fscr0"], writes=["awbw"])
        self.dma("SPD", t["b1q"][:, :], d["b1r"][:, :], writes=["b1q"])
        self.dma("SPD", t["b2q"][:, :], d["b2r"][:, :], writes=["b2q"])
        E("DVE", lambda h: h.memset(t["ones_row"][:, :], 1.0), writes=["ones_row"])
        E("DVE", lambda h: h.memset(t["ones_colb"][:, :], 1.0), writes=["ones_colb"])
        E("DVE", lambda h: h.memset(t["gz"][:, :], 0.0), writes=["gz"])
        E("DVE", lambda h: h.memset(t["ind"][:, :], 1.0), writes=["ind"])
        E("DVE", lambda h: h.tensor_scalar(out=t["ind"][:, :], in0=t["ind"][:, :],
                                           scalar1=float(2.0**-40), scalar2=1.0,
                                           op0=AL.add, op1=AL.subtract),
          reads=["ind"], writes=["ind"])
        E("DVE", lambda h: h.tensor_scalar(out=t["ind"][:, :], in0=t["ind"][:, :],
                                           scalar1=float(2.0**40), scalar2=None,
                                           op0=AL.mult),
          reads=["ind"], writes=["ind"])
        for n, base in (("cv16", 2.0**42), ("cv4", 2.0**49)):
            E("DVE", lambda h, n=n, base=base: h.memset(t[n][:, :], float(base)),
              writes=[n])
            E("DVE", lambda h, n=n: h.tensor_scalar(
                out=t[n][:, :], in0=t[n][:, :], scalar1=1.0, scalar2=None,
                op0=AL.add), reads=[n], writes=[n])
        E("DVE", lambda h: h.tensor_scalar(out=t["epsp"][:, :], in0=t["ind"][:, :],
                                           scalar1=float(2.0**-30), scalar2=None,
                                           op0=AL.mult),
          reads=["ind"], writes=["epsp"])
        E("DVE", lambda h: h.tensor_scalar(out=t["epsn"][:, :], in0=t["ind"][:, :],
                                           scalar1=float(-2.0**-30), scalar2=None,
                                           op0=AL.mult),
          reads=["ind"], writes=["epsn"])
        E("ACT", lambda h: h.activation(out=t["nA"][:, :], in_=t["alog"][:, :],
                                        func=AF.Exp), reads=["alog"], writes=["nA"])
        E("DVE", lambda h: h.tensor_scalar(out=t["nA"][:, :], in0=t["nA"][:, :],
                                           scalar1=-1.0, scalar2=None, op0=AL.mult),
          reads=["nA"], writes=["nA"])
        self.sgn_q43(t["b1q"][:, :], "b1q", t["b1_8"][:, :], "b1_8", adj_w=True)
        E("DVE", lambda h: h.tensor_copy(out=t["b1q"][:, :], in_=t["b1_8"][:, :]),
          reads=["b1_8"], writes=["b1q"])
        self.sgn_q43(t["b2q"][:, :], "b2q", t["b2_8"][:, :], "b2_8", adj_w=True)
        E("DVE", lambda h: h.tensor_copy(out=t["b2q"][:, :], in_=t["b2_8"][:, :]),
          reads=["b2_8"], writes=["b2q"])
        self.enter_phase("ffn")
        astages = []
        for w, scr in [("qw", "s_qw"), ("kw", "s_kw"), ("vw", "s_vw"),
                       ("gw", "s_gw"), ("ow", "s_ow"), ("fcw", "s_fcw"),
                       ("pw", "s_pw")]:
            src = d[w]
            rows, cols = src.shape
            nst = rows * cols // (128 * 4096)
            for i in range(nst):
                astages.append((w, src, d[scr], cols, 4096 // cols, i))

        def emit_astage(st, par):
            w, src, scr_ap, cols, spc, i = st
            if par == 0:
                buf, bufr = t["x_sub"], "x_sub"
                bf, bfr, bo = t["xbfs"], "xbfs", 0
            else:
                buf, bufr = t["pwsB"], "pwsB"
                bf, bfr, bo = t["wwin0"], "w0b", 4096
            in_ap = bass.AP(src.tensor, i * spc * 128 * cols,
                            [[cols, 128], [128 * cols, spc], [1, cols]])
            self.dma("SPD", buf[:, 0:4096].rearrange("p (a c) -> p a c", c=cols),
                     in_ap, writes=[bufr])
            E("ACT", lambda h, buf=buf, bf=bf, bo=bo: h.activation(
                out=bf[:, bo:bo + 4096], in_=buf[:, 0:4096], func=AF.Copy),
              reads=[bufr], writes=[bfr])
            if w == "pw":
                for sl in range(spc):
                    hc = i * spc + sl
                    out_ap = bass.AP(scr_ap.tensor, hc * 128,
                                     [[32768, 128], [4096, 8], [1, 128]])
                    self.dma("POOLD", out_ap,
                             bf[:, bo + sl * 1024:bo + (sl + 1) * 1024].rearrange(
                                 "p (a j) -> p a j", j=128),
                             reads=[bfr])
            else:
                self.dma("POOLD", scr_ap[:, i * 4096:(i + 1) * 4096],
                         bf[:, bo:bo + 4096], reads=[bfr])

        pieces = ([("w1", sl, 0) for sl in range(32)]
                  + [("w2", sl, hq) for sl in range(8) for hq in range(4)])
        ai = 0
        for k, (w, sl, cq) in enumerate(pieces):
            par = k % 2
            if par == 0:
                zin, zr = t["pzA"], "pzA"
                scrp = ("fscr0", "fscr1")
                wbf, wo, wreg = t["wwin0"], 0, "w0a"
                pt, ptregs = p["ptb"], ["ptb"]
            else:
                zin, zr = t["pzB"], "pzB"
                scrp = ("pscr0B", "pscr1B")
                wbf, wo, wreg = t["wwin1"], 0, "w1a"
                pt, ptregs = p["ptb"], ["ptb"]
            src = d[w]
            self.dma("SPD", zin[:, 0:1024],
                     src[sl * 128:(sl + 1) * 128, cq * 1024:(cq + 1) * 1024],
                     writes=[zr])
            u8q = t["u8sl"][:, par * 1024:(par + 1) * 1024]
            self.sgn_q43(zin[:, 0:1024], zr, u8q, f"pw8:{par}", adj_w=True,
                         scr=scrp)
            E("DVE", lambda h, u8q=u8q, wbf=wbf, wo=wo: h.tensor_copy(
                out=wbf[:, wo:wo + 1024], in_=u8q),
              reads=[f"pw8:{par}"], writes=[wreg])

            def trs(h, wbf=wbf, wo=wo, pt=pt):
                for dc in range(8):
                    ins = h.transpose(pt[:, dc * 128:(dc + 1) * 128],
                                      wbf[:, wo + dc * 128:wo + (dc + 1) * 128],
                                      t["identb"][:, :])
                return ins
            E("PE", trs, reads=[wreg, "identb"], writes=ptregs)
            u8o = t["u8sl"][:, (2 + par) * 1024:(3 + par) * 1024]
            E("DVE", lambda h, u8o=u8o, pt=pt: h.tensor_copy(
                out=u8o, in_=pt[:, 0:1024]),
              reads=ptregs, writes=[f"pw8:{2 + par}"])
            if w == "w1":
                out_ap = bass.AP(d["s_w1t"].tensor, sl * 128,
                                 [[32768, 128], [4096, 8], [1, 128]])
                st_src = u8o.rearrange("p (a j) -> p a j", j=128)
            else:
                out_ap = bass.AP(d["s_w2t"].tensor, sl * 4096 + cq * 1024,
                                 [[32768, 128], [1, 1024]])
                st_src = u8o
            self.dma("POOLD", out_ap, st_src, reads=[f"pw8:{2 + par}"])
            if k % 5 in (0, 2) and ai < len(astages):
                emit_astage(astages[ai], ai % 2)
                ai += 1
        while ai < len(astages):
            emit_astage(astages[ai], ai % 2)
            ai += 1
        self.em.merge_regions(["pzA", "pzB", "pscr0B", "pscr1B"], "h8sl")
        self.em.merge_regions(["pw8:0", "pw8:1", "pw8:2", "pw8:3"], "u8sl")
        self.em.merge_regions(["pwsB"], "f8w0")
        self.em.merge_regions(["pwsB"], "f8w1")

    def _sx_ap(self, sub):
        return bass.AP(self.d["s_x"].tensor, sub * 512,
                       [[2048, 128], [128 * 2048, 8], [1, 512]])

    def load_x_tile(self, seq):
        t, d, p, E = self.t, self.d, self.p, self.E
        for tc in range(16):
            self.dma("SPD", t["x_sub"][:, 0:1024],
                     d["x"][seq, tc * 128:(tc + 1) * 128, :], writes=["x_sub"])
            def trs(h):
                for dc in range(8):
                    ins = h.transpose(p["pm0"][:, dc * 128:(dc + 1) * 128],
                                      t["x_sub"][:, dc * 128:(dc + 1) * 128],
                                      t["identf"][:, :])
                return ins
            E("PE", trs, reads=["x_sub", "identf"],
              writes=[f"pm0:{i}" for i in range(8)])
            E("ACT", lambda h: h.activation(out=t["x_sub"][:, 1024:2048],
                                            in_=p["pm0"][:, :], func=AF.Copy),
              reads=[f"pm0:{i}" for i in range(8)], writes=["x_sub"])
            out_ap = bass.AP(d["s_x"].tensor, tc * 128,
                             [[2048, 128], [128 * 2048, 8], [1, 128]])
            self.dma("POOLD", out_ap,
                     t["x_sub"][:, 1024:2048].rearrange("p (a j) -> p a j", j=128),
                     reads=["x_sub"], writes=["s_x"])

    def store_x_tile(self, seq):
        t, d, p, E = self.t, self.d, self.p, self.E
        for tc in range(16):
            in_ap = bass.AP(d["s_x"].tensor, tc * 128,
                            [[2048, 128], [128 * 2048, 8], [1, 128]])
            self.dma("SPD", t["x_sub"][:, 0:1024].rearrange(
                "p (a j) -> p a j", j=128), in_ap,
                reads=["s_x"], writes=["x_sub"])
            def trs(h):
                for dc in range(8):
                    ins = h.transpose(p["pm0"][:, dc * 128:(dc + 1) * 128],
                                      t["x_sub"][:, dc * 128:(dc + 1) * 128],
                                      t["identf"][:, :])
                return ins
            E("PE", trs, reads=["x_sub", "identf"],
              writes=[f"pm0:{i}" for i in range(8)])
            E("ACT", lambda h: h.activation(out=t["x_sub"][:, 1024:2048],
                                            in_=p["pm0"][:, :], func=AF.Copy),
              reads=[f"pm0:{i}" for i in range(8)], writes=["x_sub"])
            self.dma("POOLD", d["y"][seq, tc * 128:(tc + 1) * 128, :],
                     t["x_sub"][:, 1024:2048], reads=["x_sub"], writes=["y"])

    def ln_stats(self, with_s=False):
        t, p, E = self.t, self.p, self.E
        E("ACT", lambda h: h.activation(out=t["xbfs"][:, :], in_=t["x_sub"][:, 0:4096],
                                        func=AF.Copy),
          reads=["x_sub"], writes=["xbfs"])
        def mmsum(h):
            for dc in range(8):
                ins = h.matmul(p["pb1"][0:1, :], t["ones_colb"][:, :],
                               t["xbfs"][:, dc * 512:(dc + 1) * 512],
                               start=(dc == 0), stop=(dc == 7))
            return ins
        E("PE", mmsum, reads=["xbfs", "ones_colb"], writes=["pb1"])
        E("ACT", lambda h: h.activation(out=t["xbfs"][:, :], in_=t["xbfs"][:, :],
                                        func=AF.Square),
          reads=["xbfs"], writes=["xbfs"])
        def mmsq(h):
            for dc in range(8):
                ins = h.matmul(p["pb2"][0:1, :], t["ones_colb"][:, :],
                               t["xbfs"][:, dc * 512:(dc + 1) * 512],
                               start=(dc == 0), stop=(dc == 7))
            return ins
        E("PE", mmsq, reads=["xbfs", "ones_colb"], writes=["pb2"])
        E("DVE", lambda h: h.tensor_scalar(out=t["r_mu"][:, :], in0=p["pb1"][0:1, :],
                                           scalar1=float(1.0 / 1024), scalar2=None,
                                           op0=AL.mult), reads=["pb1"], writes=["r_mu"])
        E("DVE", lambda h: h.tensor_scalar(out=t["r_t1"][:, :], in0=p["pb2"][0:1, :],
                                           scalar1=float(1.0 / 1024), scalar2=1e-5,
                                           op0=AL.mult, op1=AL.add),
          reads=["pb2"], writes=["r_t1"])
        E("DVE", lambda h: h.tensor_mul(out=t["r_t2"][:, :], in0=t["r_mu"][:, :],
                                        in1=t["r_mu"][:, :]),
          reads=["r_mu"], writes=["r_t2"])
        E("DVE", lambda h: h.tensor_sub(out=t["r_t1"][:, :], in0=t["r_t1"][:, :],
                                        in1=t["r_t2"][:, :]),
          reads=["r_t1", "r_t2"], writes=["r_t1"])
        E("ACT", lambda h: h.activation(out=t["r_t2"][:, :], in_=t["r_t1"][:, :],
                                        func=AF.Ln), reads=["r_t1"], writes=["r_t2"])
        E("ACT", lambda h: h.activation(out=t["r_inv"][:, :], in_=t["r_t2"][:, :],
                                        func=AF.Exp, scale=-0.5),
          reads=["r_t2"], writes=["r_inv"])
        if with_s:
            E("DVE", lambda h: h.reciprocal(out=t["r_t2"][:, :], in_=t["r_t1"][:, :]),
              reads=["r_t1"], writes=["r_t2"])
            E("DVE", lambda h: h.tensor_scalar(out=t["r_t2"][:, :], in0=t["r_t2"][:, :],
                                               scalar1=float(-1e-5),
                                               scalar2=float(1.0 + 1e-6),
                                               op0=AL.mult, op1=AL.add),
              reads=["r_t2"], writes=["r_t2"])
            E("ACT", lambda h: h.activation(out=t["r_t2"][:, :], in_=t["r_t2"][:, :],
                                            func=AF.Ln), reads=["r_t2"], writes=["r_t2"])
            E("ACT", lambda h: h.activation(out=t["r_t2"][:, :], in_=t["r_t2"][:, :],
                                            func=AF.Exp, scale=-0.5),
              reads=["r_t2"], writes=["r_t2"])
            E("DVE", lambda h: h.tensor_mul(out=t["r_s"][:, :], in0=t["r_inv"][:, :],
                                            in1=t["r_t2"][:, :]),
              reads=["r_inv", "r_t2"], writes=["r_s"])

    def bc8(self, name, col0=0, rep=128):
        ap = self.t[name][:, :]
        return bass.AP(ap.tensor, col0, [[ap.shape[1], 128], [1, 8], [0, rep]])

    def bcast_row(self, row_ap, psname, width=512, col0=0, reads=()):
        t, p, E = self.t, self.p, self.E
        if psname.startswith("pm"):
            wr = [f"{psname}:{i}" for i in
                  range(col0 // 128, (col0 + width + 127) // 128)]
        else:
            wr = [psname]
        E("PE", lambda h: h.matmul(p[psname][:, col0:col0 + width],
                                   t["ones_row"][:, :], row_ap,
                                   start=True, stop=True),
          reads=list(reads) + ["ones_row"], writes=wr)

    def ffn_half(self, seq):
        import os
        dbg = os.environ.get("ANT_DBG", "")
        t, d, p, E = self.t, self.d, self.p, self.E
        self.enter_phase("ffn")
        for sub in range(NSUB):
            self.dma("SPD", t["x_sub"][:, :].rearrange("p (a j) -> p a j", j=512),
                     self._sx_ap(sub), reads=["s_x"], writes=["x_sub"])
            self.ln_stats()
            self.bcast_row(t["r_mu"][:, :], "pb1", reads=["r_mu"])
            self.bcast_row(t["r_inv"][:, :], "pb2", reads=["r_inv"])
            for dc in range(8):
                xs = t["x_sub"][:, dc * 512:(dc + 1) * 512]
                fa, fb = (("fscr0", "fscr1") if dc % 2 == 0
                          else ("fscr2", "fscr3"))
                ta, tb = t[fa], t[fb]
                E("DVE", lambda h, xs=xs, ta=ta: h.tensor_sub(
                    out=ta[:, 0:512], in0=xs, in1=p["pb1"][:, :]),
                  reads=["x_sub", "pb1"], writes=[fa])
                E("DVE", lambda h, ta=ta, tb=tb: h.tensor_mul(
                    out=tb[:, 0:512], in0=ta[:, 0:512],
                    in1=p["pb2"][:, :]),
                  reads=[fa, "pb2"], writes=[fb])
                self.cust("ANT_QPOS", ta[:, 0:512], tb[:, 0:512],
                          s0=t["cv4"][:, :], s1=MSUB43, imm2=NEGBIG,
                          reads=[fb, "cv4"], writes=[fa])
                self.cust("ANT_LERP", t["u8sl"][:, dc * 512:(dc + 1) * 512],
                          tb[:, 0:512], in1=ta[:, 0:512],
                          s0=t["ind"][:, :],
                          reads=[fb, fa, "ind"], writes=["u8sl"])
            if dbg == "u8":
                E("DVE", lambda h: h.tensor_copy(out=t["x_sub"][:, :],
                                                 in_=t["u8sl"][:, :]),
                  reads=["u8sl"], writes=["x_sub"])
                self.dma("POOLD", self._sx_ap(sub),
                         t["x_sub"][:, :].rearrange("p (a j) -> p a j", j=512),
                         reads=["x_sub"], writes=["s_x"])
                continue
            for hq in range(4):
                w = f"f8w{hq % 2}"
                in_ap = bass.AP(d["s_w1t"].tensor, hq * 1024,
                                [[32768, 128], [4096, 8], [1, 1024]])
                self.dma("SPD", t[w][:, :].rearrange("p (a j) -> p a j", j=1024),
                         in_ap, writes=[w])
                for hcl in range(8):
                    hc = hq * 8 + hcl
                    pb = p[f"pb{hc % 2}"]
                    pbr = f"pb{hc % 2}"
                    def grp(h, w=w, hcl=hcl, pb=pb):
                        for dc in range(8):
                            ins = h.matmul(
                                pb[:, :],
                                t[w][:, dc * 1024 + hcl * 128:dc * 1024 + (hcl + 1) * 128],
                                t["u8sl"][:, dc * 512:(dc + 1) * 512],
                                start=(dc == 0), stop=(dc == 7))
                        return ins
                    E("PE", grp, reads=[w, "u8sl"], writes=[pbr])
                    sfx = "" if hc % 2 == 0 else "b"
                    rA0, rQs, rA1, rTd = ("A0" + sfx, "Qs" + sfx,
                                          "A1" + sfx, "Td" + sfx)
                    tA0, tQs, tA1, tTd = (t["A0" + sfx] if sfx else t["A0"],
                                          t["Qsb"] if sfx else t["Qs"],
                                          t["A1b"] if sfx else t["A1"],
                                          t["Tdb"] if sfx else t["Td"])
                    E("ACT", lambda h, pb=pb, hc=hc, tA0=tA0: h.activation(
                        out=tA0[:, :], in_=pb[:, :], func=AF.Relu,
                        bias=t["b1q"][:, hc:hc + 1]),
                      reads=[pbr, "b1q"], writes=[rA0])
                    self.cust("ANT_QPOS", tQs[:, :], tA0[:, :],
                              s0=t["cv16"][:, :], s1=MSUB16, imm2=float(2.0**-14),
                              reads=[rA0, "cv16"], writes=[rQs])
                    self.cust("ANT_LERP", tA1[:, :], tA0[:, :],
                              in1=tQs[:, :], s0=t["ind"][:, :],
                              reads=[rA0, rQs, "ind"], writes=[rA1])
                    self.cust("ANT_TD4S", tTd[:, :], tA1[:, :],
                              s0=float(2.0**-16), s1=float(2.0**-2), imm2=4.0,
                              reads=[rA1], writes=[rTd])
                    self.cust("ANT_NLERP", tA0[:, :], tA1[:, :],
                              in1=tTd[:, :], s0=t["epsn"][:, :],
                              reads=[rA1, rTd, "epsn"], writes=[rA0])
                    self.cust("ANT_QPOS", tQs[:, :], tA0[:, :],
                              s0=t["cv4"][:, :], s1=MSUB43, imm2=float(2.0**-6),
                              reads=[rA0, "cv4"], writes=[rQs])
                    self.cust("ANT_LERP", t["h8sl"][:, hc * 512:(hc + 1) * 512],
                              tA0[:, :], in1=tQs[:, :], s0=t["ind"][:, :],
                              reads=[rA0, rQs, "ind"], writes=["h8sl"])
            if dbg == "h8":
                E("DVE", lambda h: h.tensor_copy(out=t["x_sub"][:, :],
                                                 in_=t["h8sl"][:, 0:4096]),
                  reads=["h8sl"], writes=["x_sub"])
                self.dma("POOLD", self._sx_ap(sub),
                         t["x_sub"][:, :].rearrange("p (a j) -> p a j", j=512),
                         reads=["x_sub"], writes=["s_x"])
                continue
            for dc in range(8):
                w = f"f8w{dc % 2}"
                self.dma("SPD", t[w][:, 0:4096],
                         d["s_w2t"][:, dc * 4096:(dc + 1) * 4096], writes=[w])
                mk32, mkd = t[f"mk{2 * (dc % 2)}"], t[f"mk{2 * (dc % 2) + 1}"]
                r32, rd = f"mk{2 * (dc % 2)}", f"mk{2 * (dc % 2) + 1}"
                self.dma("POOLD", mk32[:, :],
                         d["m32f"][seq, dc, :, sub * 512:(sub + 1) * 512],
                         writes=[r32])
                self.dma("POOLD", mkd[:, :],
                         d["mdf"][seq, dc, :, sub * 512:(sub + 1) * 512],
                         writes=[rd])
                pb = p[f"pb{dc % 2}"]
                pbr = f"pb{dc % 2}"
                def grp2(h, w=w, pb=pb):
                    for hc in range(32):
                        ins = h.matmul(pb[:, :],
                                       t[w][:, hc * 128:(hc + 1) * 128],
                                       t["h8sl"][:, hc * 512:(hc + 1) * 512],
                                       start=(hc == 0), stop=(hc == 31))
                    return ins
                E("PE", grp2, reads=[w, "h8sl"], writes=[pbr])
                sfx = "" if dc % 2 == 0 else "b"
                rA0, rQs, rA1 = "A0" + sfx, "Qs" + sfx, "A1" + sfx
                tA0 = t["A0b"] if sfx else t["A0"]
                tQs = t["Qsb"] if sfx else t["Qs"]
                tA1 = t["A1b"] if sfx else t["A1"]
                fa = "fscr0" if dc % 2 == 0 else "fscr2"
                ta = t[fa]
                mblend = "Td" if dc % 2 == 0 else "Tdb"
                tmb = t[mblend]
                E("ACT", lambda h, pb=pb, dc=dc, tA0=tA0: h.activation(
                    out=tA0[:, :], in_=pb[:, :], func=AF.Identity,
                    bias=t["b2q"][:, dc:dc + 1], scale=1.0),
                  reads=[pbr, "b2q"], writes=[rA0])
                self.cust("ANT_QPOS", tQs[:, :], tA0[:, :],
                          s0=t["cv16"][:, :], s1=MSUB16, imm2=NEGBIG,
                          reads=[rA0, "cv16"], writes=[rQs])
                self.cust("ANT_LERP", tA1[:, :], tA0[:, :],
                          in1=tQs[:, :], s0=t["ind"][:, :],
                          reads=[rA0, rQs, "ind"], writes=[rA1])
                E("DVE", lambda h, mkd=mkd, mk32=mk32, tmb=tmb: h.scalar_tensor_tensor(
                    out=tmb[:, :], in0=mkd[:, :], scalar=t["ind"][:, :],
                    in1=mk32[:, :], op0=AL.mult, op1=AL.add),
                  reads=[rd, r32, "ind"], writes=[mblend])
                E("DVE", lambda h, ta=ta, tA1=tA1, tmb=tmb: h.tensor_mul(
                    out=ta[:, 0:512], in0=tA1[:, :], in1=tmb[:, :]),
                  reads=[rA1, mblend], writes=[fa])
                E("DVE", lambda h, dc=dc, ta=ta: h.tensor_add(
                    out=t["x_sub"][:, dc * 512:(dc + 1) * 512],
                    in0=t["x_sub"][:, dc * 512:(dc + 1) * 512],
                    in1=ta[:, 0:512]),
                  reads=["x_sub", fa], writes=["x_sub"])
            self.dma("POOLD", self._sx_ap(sub),
                     t["x_sub"][:, :].rearrange("p (a j) -> p a j", j=512),
                     reads=["x_sub"], writes=["s_x"])

    def _win(self, full=True):
        if not hasattr(self, "_wslot"):
            self._wslot = 0
        if full:
            if self._wslot % 2:
                self._wslot += 1
            s = (self._wslot // 2) % 2
            self._wslot += 2
            return self.t[f"wwin{s}"], 0, [f"w{s}a", f"w{s}b"]
        s = self._wslot % 4
        self._wslot += 1
        return self.t[f"wwin{s // 2}"], (s % 2) * 4096, [f"w{s // 2}{'ab'[s % 2]}"]

    def attn(self, seq):
        import os
        asub = os.environ.get("ANT_ATTN", "full")
        t, d, p, E = self.t, self.d, self.p, self.E
        E("DVE", lambda h: h.memset(t["S0"][:, :], 0.0), writes=["S0"])
        E("DVE", lambda h: h.memset(t["S0b"][:, :], 0.0), writes=["S0b"])
        E("DVE", lambda h: h.memset(t["gprev"][:, :], 0.0), writes=["gprev"])
        for sub in range(NSUB):
            self.enter_phase("attn_gen")
            self.dma("SPD", t["x_sub"][:, :].rearrange("p (a j) -> p a j", j=512),
                     self._sx_ap(sub), reads=["s_x"], writes=["x_sub"])
            self.ln_stats(with_s=True)
            self.bcast_row(t["r_mu"][:, :], "pb1", reads=["r_mu"])
            self.bcast_row(t["r_s"][:, :], "pb2", reads=["r_s"])
            for dc in range(8):
                xs = t["x_sub"][:, dc * 512:(dc + 1) * 512]
                if dc % 2 == 0:
                    s0, s0r = t["fscr0"][:, 0:512], "fscr0"
                    s1, s1r = t["fscr1"][:, 0:512], "fscr1"
                else:
                    s0, s0r = t["Esl"][:, 0:512], "Esl"
                    s1, s1r = t["Esl"][:, 512:1024], "Esl"
                E("DVE", lambda h, xs=xs, s0=s0: h.tensor_sub(
                    out=s0, in0=xs, in1=p["pb1"][:, :]),
                  reads=["x_sub", "pb1"], writes=[s0r])
                E("DVE", lambda h, s0=s0, s1=s1: h.tensor_mul(
                    out=s1, in0=s0, in1=p["pb2"][:, :]),
                  reads=[s0r, "pb2"], writes=[s1r])
                E("ACT", lambda h, dc=dc, s1=s1: h.activation(
                    out=t["hsl"][:, dc * 512:(dc + 1) * 512],
                    in_=s1, func=AF.Identity,
                    scale=t["ln1c"][:, dc:dc + 1]),
                  reads=[s1r, "ln1c"], writes=["hsl"])
            if asub == "h":
                continue
            for unit, slab in (("s_qw", "qsl"), ("s_kw", "ksl"),
                               ("s_vw", "vsl"), ("s_gw", "gsl")):
                win, off, wr = self._win(full=True)
                self.dma("SPD", win[:, :], d[unit][:, :], writes=wr)
                for oc in range(8):
                    pb = p[f"pb{oc % 2}"]
                    pbr = f"pb{oc % 2}"
                    def grp(h, win=win, oc=oc, pb=pb):
                        for dc in range(8):
                            ins = h.matmul(
                                pb[:, :],
                                win[:, dc * 1024 + oc * 128:dc * 1024 + (oc + 1) * 128],
                                t["hsl"][:, dc * 512:(dc + 1) * 512],
                                start=(dc == 0), stop=(dc == 7))
                        return ins
                    E("PE", grp, reads=wr + ["hsl"], writes=[pbr])
                    E("ACT", lambda h, pb=pb, slab=slab, oc=oc: h.activation(
                        out=t[slab][:, oc * 512:(oc + 1) * 512], in_=pb[:, :],
                        func=AF.Silu), reads=[pbr], writes=[slab])
            E("ACT", lambda h: h.activation(
                out=t["gsl"][:, :], in_=t["gsl"][:, :], func=AF.Identity,
                scale=t["onormc"][:, :]),
              reads=["gsl", "onormc"], writes=["gsl"])
            if asub == "proj":
                continue
            for part, (c0, outr) in (("a", (0, "r_za")), ("b", (8, "r_zb"))):
                def grpab(h, c0=c0):
                    for dc in range(8):
                        ins = h.matmul(
                            p["pb0"][0:8, :],
                            t["awbw"][:, dc * 16 + c0:dc * 16 + c0 + 8],
                            t["hsl"][:, dc * 512:(dc + 1) * 512],
                            start=(dc == 0), stop=(dc == 7))
                    return ins
                E("PE", grpab, reads=["awbw", "hsl"], writes=["pb0"])
                if part == "a":
                    E("ACT", lambda h, outr=outr: h.activation(
                        out=t[outr][:, :], in_=p["pb0"][0:8, :], func=AF.Identity,
                        bias=t["dtb"][:, :]), reads=["pb0", "dtb"], writes=[outr])
                else:
                    E("ACT", lambda h, outr=outr: h.activation(
                        out=t[outr][:, :], in_=p["pb0"][0:8, :], func=AF.Copy),
                      reads=["pb0"], writes=[outr])
            for slab, row in (("qsl", "qn"), ("ksl", "kn")):
                E("DVE", lambda h, slab=slab: h.tensor_mul(
                    out=t["xbfs"][:, :], in0=t[slab][:, :], in1=t[slab][:, :]),
                  reads=[slab], writes=["xbfs"])
                def grpe8(h):
                    for hh in range(8):
                        ins = h.matmul(p["pb2"][0:8, :], t["e8"][:, hh * 8:(hh + 1) * 8],
                                       t["xbfs"][:, hh * 512:(hh + 1) * 512],
                                       start=(hh == 0), stop=(hh == 7))
                    return ins
                E("PE", grpe8, reads=["xbfs", "e8"], writes=["pb2"])
                E("DVE", lambda h, row=row: h.tensor_scalar(
                    out=t[row][:, :], in0=p["pb2"][0:8, :], scalar1=float(1e-6),
                    scalar2=None, op0=AL.add), reads=["pb2"], writes=[row])
                E("ACT", lambda h, row=row: h.activation(
                    out=t[row][:, :], in_=t[row][:, :], func=AF.Ln),
                  reads=[row], writes=[row])
                E("ACT", lambda h, row=row: h.activation(
                    out=t[row][:, :], in_=t[row][:, :], func=AF.Exp, scale=-0.5),
                  reads=[row], writes=[row])
            for hh in range(8):
                E("PE", lambda h, hh=hh: h.matmul(
                    p["pb1"][:, :], t["sel8"][:, hh * 128:(hh + 1) * 128],
                    t["kn"][:, :], start=True, stop=True),
                  reads=["sel8", "kn"], writes=["pb1"])
                E("DVE", lambda h, hh=hh: h.tensor_mul(
                    out=t["ksl"][:, hh * 512:(hh + 1) * 512],
                    in0=t["ksl"][:, hh * 512:(hh + 1) * 512], in1=p["pb1"][:, :]),
                  reads=["ksl", "pb1"], writes=["ksl"])
            def qnt(h):
                for c in range(4):
                    ins = h.transpose(p["pm0"][:, c * 8:(c + 1) * 8],
                                      t["qn"][0:8, c * 128:(c + 1) * 128],
                                      t["identf"][0:8, 0:8])
                return ins
            E("PE", qnt, reads=["qn", "identf"], writes=["pm0:0"])
            E("ACT", lambda h: h.activation(out=t["qnT"][:, :],
                                            in_=p["pm0"][:, 0:32], func=AF.Copy),
              reads=["pm0:0"], writes=["qnT"])
            E("ACT", lambda h: h.activation(out=t["r_beta"][:, :], in_=t["r_zb"][:, :],
                                            func=AF.Sigmoid),
              reads=["r_zb"], writes=["r_beta"])
            for src, dst, extra in (("r_za", "r_g", "softplus"),
                                    ("r_zb", "r_lb", "logsig")):
                E("ACT", lambda h, src=src: h.activation(
                    out=t["r_sca"][:, :], in_=t[src][:, :], func=AF.Abs),
                  reads=[src], writes=["r_sca"])
                E("ACT", lambda h: h.activation(out=t["r_sca"][:, :],
                                                in_=t["r_sca"][:, :], func=AF.Exp,
                                                scale=-1.0),
                  reads=["r_sca"], writes=["r_sca"])
                E("ACT", lambda h: h.activation(out=t["r_sca"][:, :],
                                                in_=t["r_sca"][:, :], func=AF.Ln,
                                                bias=1.0),
                  reads=["r_sca"], writes=["r_sca"])
                if extra == "softplus":
                    pass
                if extra == "softplus":
                    E("DVE", lambda h, src=src: h.tensor_scalar(
                        out=t["r_scb"][:, :], in0=t[src][:, :], scalar1=0.0,
                        scalar2=None, op0=AL.max), reads=[src], writes=["r_scb"])
                    E("DVE", lambda h: h.tensor_add(out=t["r_g"][:, :],
                                                    in0=t["r_sca"][:, :],
                                                    in1=t["r_scb"][:, :]),
                      reads=["r_sca", "r_scb"], writes=["r_g"])
                    E("DVE", lambda h: h.tensor_scalar(
                        out=t["r_g"][:, :], in0=t["r_g"][:, :],
                        scalar1=t["nA"][:, :], scalar2=None, op0=AL.mult),
                      reads=["r_g", "nA"], writes=["r_g"])
                else:
                    E("DVE", lambda h, src=src: h.tensor_scalar(
                        out=t["r_scb"][:, :], in0=t[src][:, :], scalar1=0.0,
                        scalar2=None, op0=AL.min), reads=[src], writes=["r_scb"])
                    E("DVE", lambda h: h.tensor_sub(out=t["r_lb"][:, :],
                                                    in0=t["r_scb"][:, :],
                                                    in1=t["r_sca"][:, :]),
                      reads=["r_scb", "r_sca"], writes=["r_lb"])
            E("DVE", lambda h, sub=sub: h.tensor_tensor_scan(
                out=t["gtile"][0:8, sub * 512:(sub + 1) * 512],
                data0=t["r_g"][:, :], data1=t["r_g"][:, :],
                initial=t["gprev"][0:8, 0:1], op0=AL.add, op1=AL.bypass),
              reads=["r_g", "gprev"], writes=["gtile"])
            E("DVE", lambda h, sub=sub: h.tensor_copy(
                out=t["gprev"][0:8, 0:1],
                in_=t["gtile"][0:8, sub * 512 + 511:sub * 512 + 512]),
              reads=["gtile"], writes=["gprev"])
            E("DVE", lambda h, sub=sub: h.tensor_add(
                out=t["r_glb"][:, :], in0=t["gtile"][0:8, sub * 512:(sub + 1) * 512],
                in1=t["r_lb"][:, :]), reads=["gtile", "r_lb"], writes=["r_glb"])
            if asub == "rows":
                continue
            nch = NCH_SUB if not asub.startswith("mixer") else int(asub[5:])
            for c in range(min(nch, NCH_SUB)):
                self._mixer_chunk(seq, sub, c)
            if asub.startswith("mixer"):
                continue
            if os.environ.get("ANT_DBG", "") in ("X0", "TRI", "ESL", "MT",
                                                 "Ymat", "Rmat", "Omat"):
                continue
            if os.environ.get("ANT_DBG", "") == "o2f":
                E("DVE", lambda h: h.tensor_copy(out=t["x_sub"][:, :],
                                                 in_=t["o2f"][:, :]),
                  reads=["o2f"], writes=["x_sub"])
                self.dma("POOLD", self._sx_ap(sub),
                         t["x_sub"][:, :].rearrange("p (a j) -> p a j", j=512),
                         reads=["x_sub"], writes=["s_x"])
                continue
            self._post_stage(seq, sub)
            self.dma("POOLD", self._sx_ap(sub),
                     t["x_sub"][:, :].rearrange("p (a j) -> p a j", j=512),
                     reads=["x_sub"], writes=["s_x"])

    def _mixer_chunk(self, seq, sub, c):
        import os
        amx = os.environ.get("ANT_MIX", "full")
        t, d, p, E = self.t, self.d, self.p, self.E
        dbg = os.environ.get("ANT_DBG", "")
        def dump(slab, reads):
            if seq == 0 and sub == 0 and c == 0:
                E("DVE", lambda h: h.tensor_copy(out=t["x_sub"][:, 0:1024],
                                                 in_=t[slab][:, :]),
                  reads=reads, writes=["x_sub"])
                self.dma("POOLD", self._sx_ap(0),
                         t["x_sub"][:, :].rearrange("p (a j) -> p a j", j=512),
                         reads=["x_sub"], writes=["s_x"])
        tok0 = sub * 512 + c * 128
        PM0 = [f"pm0:{i}" for i in range(8)]
        PM1 = [f"pm1:{i}" for i in range(8)]
        base = t["gz"][0:8, 0:1] if tok0 == 0 else t["gtile"][0:8, tok0 - 1:tok0]
        baser = "gz" if tok0 == 0 else "gtile"
        E("DVE", lambda h: h.tensor_scalar(
            out=t["r_sca"][0:8, 0:128], in0=t["gtile"][0:8, tok0:tok0 + 128],
            scalar1=base, scalar2=None, op0=AL.subtract),
          reads=["gtile", baser], writes=["r_sca"])
        E("DVE", lambda h, c=c: h.tensor_scalar(
            out=t["r_scb"][0:8, 0:128], in0=t["r_glb"][0:8, c * 128:(c + 1) * 128],
            scalar1=base, scalar2=None, op0=AL.subtract),
          reads=["r_glb", baser], writes=["r_scb"])
        def trg(h, c=c):
            h.transpose(p["pm0"][:, 0:8], t["r_sca"][0:8, 0:128],
                        t["identf"][0:8, 0:8])
            h.transpose(p["pm0"][:, 8:16], t["r_scb"][0:8, 0:128],
                        t["identf"][0:8, 0:8])
            h.transpose(p["pm0"][:, 16:24], t["r_beta"][0:8, c * 128:(c + 1) * 128],
                        t["identf"][0:8, 0:8])
            return h.transpose(p["pm0"][0:1, 24:32], t["r_sca"][0:8, 127:128],
                               t["identf"][0:8, 0:8])
        E("PE", trg, reads=["r_sca", "r_scb", "r_beta", "identf"], writes=["pm0:0"])
        E("ACT", lambda h: h.activation(out=t["gcolT"][:, :], in_=p["pm0"][:, 0:32],
                                        func=AF.Copy),
          reads=["pm0:0"], writes=["gcolT"])
        E("ACT", lambda h: h.activation(out=t["gamc"][:, :], in_=t["gcolT"][:, 0:8],
                                        func=AF.Exp), reads=["gcolT"], writes=["gamc"])
        self.bcast_row(t["gcolT"][0:1, 24:32], "pm1", width=8, col0=0,
                       reads=["gcolT"])
        E("DVE", lambda h: h.tensor_sub(out=t["dcy"][:, :], in0=p["pm1"][:, 0:8],
                                        in1=t["gcolT"][:, 0:8]),
          reads=["pm1:0", "gcolT"], writes=["dcy"])
        E("ACT", lambda h: h.activation(out=t["dcy"][:, :], in_=t["dcy"][:, :],
                                        func=AF.Exp), reads=["dcy"], writes=["dcy"])
        E("ACT", lambda h: h.activation(out=t["gCb"][:, :], in_=p["pm1"][:, 0:8],
                                        func=AF.Exp), reads=["pm1:0"], writes=["gCb"])
        E("DVE", lambda h: h.tensor_mul(out=t["ndcy"][:, :], in0=t["gamc"][:, :],
                                        in1=t["gcolT"][:, 16:24]),
          reads=["gamc", "gcolT"], writes=["ndcy"])
        E("DVE", lambda h: h.tensor_scalar(out=t["ndcy"][:, :], in0=t["ndcy"][:, :],
                                           scalar1=-1.0, scalar2=None, op0=AL.mult),
          reads=["ndcy"], writes=["ndcy"])
        if amx == "g":
            return

        def ch(slab, hh):
            return t[slab][:, hh * 512 + c * 128:hh * 512 + (c + 1) * 128]

        def bc_rows(h, src):
            for hh in range(8):
                ins = h.matmul(p["pm1"][:, hh * 128:(hh + 1) * 128],
                               t["sel8"][:, hh * 128:(hh + 1) * 128],
                               src, start=True, stop=True)
            return ins
        E("PE", lambda h: bc_rows(h, t["r_scb"][0:8, 0:128]),
          reads=["sel8", "r_scb"], writes=PM1)
        gb0 = self.bc8("gcolT", 0)
        E("DVE", lambda h: h.tensor_sub(out=t["Esl"][:, :], in0=p["pm1"][:, :],
                                        in1=gb0),
          reads=PM1 + ["gcolT"], writes=["Esl"])
        E("DVE", lambda h: h.tensor_scalar(out=t["Esl"][:, :], in0=t["Esl"][:, :],
                                           scalar1=0.0, scalar2=None, op0=AL.min),
          reads=["Esl"], writes=["Esl"])
        E("ACT", lambda h: h.activation(out=t["Esl"][:, :], in_=t["Esl"][:, :],
                                        func=AF.Exp), reads=["Esl"], writes=["Esl"])
        E("DVE", lambda h: h.tensor_mul(out=t["Esl"][:, :], in0=t["Esl"][:, :],
                                        in1=t["trineg"][:, :]),
          reads=["Esl", "trineg"], writes=["Esl"])
        if dbg == "ESL":
            dump("Esl", ["Esl"])
            return
        def kkmm(h):
            for hh in range(8):
                ins = h.matmul(p["pm0"][:, hh * 128:(hh + 1) * 128],
                               ch("ksl", hh), ch("ksl", hh), start=True, stop=True)
            return ins
        E("PE", kkmm, reads=["ksl"], writes=PM0)
        E("DVE", lambda h: h.tensor_mul(out=t["Xa"][:, :], in0=t["Esl"][:, :],
                                        in1=p["pm0"][:, :]),
          reads=["Esl"] + PM0, writes=["Xa"])
        if dbg == "X0":
            dump("Xa", ["Xa"])
            return
        if dbg == "TRI":
            dump("trineg", ["trineg"])
            return
        if amx == "gx":
            return
        E("PE", lambda h: bc_rows(h, t["r_sca"][0:8, 0:128]),
          reads=["sel8", "r_sca"], writes=PM1)
        E("DVE", lambda h: h.tensor_sub(out=t["Esl"][:, :], in0=p["pm1"][:, :],
                                        in1=gb0),
          reads=PM1 + ["gcolT"], writes=["Esl"])
        E("DVE", lambda h: h.tensor_scalar(out=t["Esl"][:, :], in0=t["Esl"][:, :],
                                           scalar1=0.0, scalar2=None, op0=AL.min),
          reads=["Esl"], writes=["Esl"])
        E("ACT", lambda h: h.activation(out=t["Esl"][:, :], in_=t["Esl"][:, :],
                                        func=AF.Exp), reads=["Esl"], writes=["Esl"])
        E("DVE", lambda h: h.tensor_mul(out=t["Esl"][:, :], in0=t["Esl"][:, :],
                                        in1=t["trinc"][:, :]),
          reads=["Esl", "trinc"], writes=["Esl"])
        def qkmm(h):
            for hh in range(8):
                ins = h.matmul(p["pm0"][:, hh * 128:(hh + 1) * 128],
                               ch("ksl", hh), ch("qsl", hh), start=True, stop=True)
            return ins
        E("PE", qkmm, reads=["ksl", "qsl"], writes=PM0)
        E("DVE", lambda h: h.tensor_mul(out=t["Ab"][:, :], in0=t["Esl"][:, :],
                                        in1=p["pm0"][:, :]),
          reads=["Esl"] + PM0, writes=["Ab"])
        if amx == "ab":
            return
        def xt0(h):
            for hh in range(8):
                ins = h.transpose(p["ptb"][:, hh * 128:(hh + 1) * 128],
                                  t["Xa"][:, hh * 128:(hh + 1) * 128],
                                  t["identb"][:, :])
            return ins
        E("PE", xt0, reads=["Xa", "identb"], writes=["ptb"])
        E("DVE", lambda h: h.tensor_copy(out=t["XTa"][:, :], in_=p["ptb"][:, :]),
          reads=["ptb"], writes=["XTa"])
        E("DVE", lambda h: h.tensor_add(out=t["Zb"][:, :], in0=t["Xa"][:, :],
                                        in1=t["identr"][:, :]),
          reads=["Xa", "identr"], writes=["Zb"])
        if amx == "mt0":
            return
        NLEV = 2
        names = {"X": "Xa", "XT": "XTa", "Xn": "Xb", "XTn": "XTb"}
        for lev in range(1, NLEV + 1):
            last = lev == NLEV
            X, XT, Xn, XTn = names["X"], names["XT"], names["Xn"], names["XTn"]
            if not last:
                def sq1(h, X=X, XT=XT):
                    for hh in range(8):
                        ins = h.matmul(p["pm0"][:, hh * 128:(hh + 1) * 128],
                                       t[XT][:, hh * 128:(hh + 1) * 128],
                                       t[X][:, hh * 128:(hh + 1) * 128],
                                       start=True, stop=True)
                    return ins
                E("PE", sq1, reads=[X, XT], writes=PM0)
            def sq2(h, X=X, XT=XT):
                for hh in range(8):
                    ins = h.matmul(p["pm1"][:, hh * 128:(hh + 1) * 128],
                                   t[X][:, hh * 128:(hh + 1) * 128],
                                   t[XT][:, hh * 128:(hh + 1) * 128],
                                   start=True, stop=True)
                return ins
            E("PE", sq2, reads=[X, XT], writes=PM1)
            if not last:
                E("ACT", lambda h, Xn=Xn: h.activation(out=t[Xn][:, :],
                                                       in_=p["pm0"][:, :],
                                                       func=AF.Copy),
                  reads=PM0, writes=[Xn])
            E("DVE", lambda h, XTn=XTn: h.tensor_copy(out=t[XTn][:, :],
                                                      in_=p["pm1"][:, :]),
              reads=PM1, writes=[XTn])
            def mtmm(h, XTn=XTn):
                for hh in range(8):
                    h.matmul(p["pm0"][:, hh * 128:(hh + 1) * 128],
                             t[XTn][:, hh * 128:(hh + 1) * 128],
                             t["Zb"][:, hh * 128:(hh + 1) * 128],
                             start=True, stop=False)
                h.matmul(p["pm0"][:, 0:512], t["identb"][:, :], t["Zb"][:, 0:512],
                         start=False, stop=False)
                return h.matmul(p["pm0"][:, 512:1024], t["identb"][:, :],
                                t["Zb"][:, 512:1024], start=False, stop=True)
            E("PE", mtmm, reads=[XTn, "Zb", "identb"], writes=PM0)
            E("ACT", lambda h: h.activation(out=t["Zb"][:, :], in_=p["pm0"][:, :],
                                            func=AF.Copy), reads=PM0, writes=["Zb"])
            names["X"], names["Xn"] = names["Xn"], names["X"]
            names["XT"], names["XTn"] = names["XTn"], names["XT"]
        if dbg == "MT":
            dump("Zb", ["Zb"])
            return
        if amx == "sq":
            return
        def ks0(h):
            for hh in range(8):
                ins = h.matmul(p["pm1"][:, hh * 128:(hh + 1) * 128],
                               ch("ksl", hh), t["S0b"][:, hh * 128:(hh + 1) * 128],
                               start=True, stop=True)
            return ins
        E("PE", ks0, reads=["ksl", "S0b"], writes=PM1)
        def vtr(h):
            for hh in range(8):
                ins = h.transpose(p["ptb"][:, hh * 128:(hh + 1) * 128],
                                  ch("vsl", hh), t["identb"][:, :])
            return ins
        E("PE", vtr, reads=["vsl", "identb"], writes=["ptb"])
        E("DVE", lambda h: h.tensor_mul(out=t["oscr"][:, :], in0=p["pm1"][:, :],
                                        in1=self.bc8("ndcy")),
          reads=PM1 + ["ndcy"], writes=["oscr"])
        E("DVE", lambda h: h.tensor_mul(out=t["otok"][:, :], in0=p["ptb"][:, :],
                                        in1=self.bc8("gcolT", 16)),
          reads=["ptb", "gcolT"], writes=["otok"])
        E("DVE", lambda h: h.tensor_add(out=t["Xa"][:, :], in0=t["otok"][:, :],
                                        in1=t["oscr"][:, :]),
          reads=["otok", "oscr"], writes=["Xa"])
        if dbg == "Ymat":
            dump("Xa", ["Xa"])
            return
        if amx == "y":
            return
        def rmm(h):
            for hh in range(8):
                ins = h.matmul(p["pm0"][:, hh * 128:(hh + 1) * 128],
                               t["Zb"][:, hh * 128:(hh + 1) * 128],
                               t["Xa"][:, hh * 128:(hh + 1) * 128],
                               start=True, stop=True)
            return ins
        E("PE", rmm, reads=["Zb", "Xa"], writes=PM0)
        E("ACT", lambda h: h.activation(out=t["Xb"][:, :], in_=p["pm0"][:, :],
                                        func=AF.Copy), reads=PM0, writes=["Xb"])
        if dbg == "Rmat":
            dump("Xb", ["Xb"])
            return
        if amx == "r":
            return
        def omm(h):
            for hh in range(8):
                ins = h.matmul(p["pm0"][:, hh * 128:(hh + 1) * 128],
                               t["Ab"][:, hh * 128:(hh + 1) * 128],
                               t["Xb"][:, hh * 128:(hh + 1) * 128],
                               start=True, stop=True)
            return ins
        E("PE", omm, reads=["Ab", "Xb"], writes=PM0)
        def qs0(h):
            for hh in range(8):
                ins = h.matmul(p["pm1"][:, hh * 128:(hh + 1) * 128],
                               ch("qsl", hh), t["S0b"][:, hh * 128:(hh + 1) * 128],
                               start=True, stop=True)
            return ins
        E("PE", qs0, reads=["qsl", "S0b"], writes=PM1)
        E("DVE", lambda h: h.tensor_mul(out=t["oscr"][:, :], in0=p["pm1"][:, :],
                                        in1=self.bc8("gamc")),
          reads=PM1 + ["gamc"], writes=["oscr"])
        E("DVE", lambda h: h.tensor_add(out=t["otok"][:, :], in0=t["oscr"][:, :],
                                        in1=p["pm0"][:, :]),
          reads=["oscr"] + PM0, writes=["otok"])
        E("DVE", lambda h, c=c: h.tensor_mul(out=t["otok"][:, :],
                                             in0=t["otok"][:, :],
                                             in1=self.bc8("qnT", c * 8)),
          reads=["otok", "qnT"], writes=["otok"])
        for hh in range(8):
            E("DVE", lambda h, hh=hh: h.scalar_tensor_tensor(
                out=t["oscr"][:, hh * 128:(hh + 1) * 128],
                in0=t["otok"][:, hh * 128:(hh + 1) * 128], scalar=1.0,
                in1=t["otok"][:, hh * 128:(hh + 1) * 128],
                op0=AL.bypass, op1=AL.mult,
                accum_out=t["osums"][:, hh:hh + 1]),
              reads=["otok"], writes=["oscr", "osums"])
        E("DVE", lambda h: h.tensor_scalar(
            out=t["osums"][:, :], in0=t["osums"][:, :],
            scalar1=float(1.0 / 128), scalar2=float(1e-6),
            op0=AL.mult, op1=AL.add), reads=["osums"], writes=["osums"])
        E("ACT", lambda h: h.activation(out=t["osums"][:, :], in_=t["osums"][:, :],
                                        func=AF.Ln), reads=["osums"], writes=["osums"])
        E("ACT", lambda h: h.activation(out=t["oscale"][:, :], in_=t["osums"][:, :],
                                        func=AF.Exp, scale=-0.5),
          reads=["osums"], writes=["oscale"])
        E("DVE", lambda h: h.tensor_mul(out=t["oscr"][:, :], in0=t["otok"][:, :],
                                        in1=self.bc8("oscale")),
          reads=["otok", "oscale"], writes=["oscr"])
        def otr(h):
            for hh in range(8):
                ins = h.transpose(p["pm1"][:, hh * 128:(hh + 1) * 128],
                                  t["oscr"][:, hh * 128:(hh + 1) * 128],
                                  t["identf"][:, :])
            return ins
        E("PE", otr, reads=["oscr", "identf"], writes=PM1)
        o2f_ap = bass.AP(t["o2f"][:, :].tensor, c * 128,
                         [[4096, 128], [512, 8], [1, 128]])
        gsl_ap = bass.AP(t["gsl"][:, :].tensor, c * 128,
                         [[4096, 128], [512, 8], [1, 128]])
        E("DVE", lambda h: h.tensor_mul(out=o2f_ap, in0=p["pm1"][:, :],
                                        in1=gsl_ap),
          reads=PM1 + ["gsl"], writes=["o2f"])
        if dbg == "Omat":
            dump("otok", ["otok"])
            return
        if amx == "o":
            return
        def ktr(h):
            for hh in range(8):
                ins = h.transpose(p["ptb"][:, hh * 128:(hh + 1) * 128],
                                  ch("ksl", hh), t["identb"][:, :])
            return ins
        E("PE", ktr, reads=["ksl", "identb"], writes=["ptb"])
        E("DVE", lambda h: h.tensor_mul(out=t["Kts"][:, :], in0=p["ptb"][:, :],
                                        in1=self.bc8("dcy")),
          reads=["ptb", "dcy"], writes=["Kts"])
        def smm(h):
            for hh in range(8):
                ins = h.matmul(p["pm0"][:, hh * 128:(hh + 1) * 128],
                               t["Kts"][:, hh * 128:(hh + 1) * 128],
                               t["Xb"][:, hh * 128:(hh + 1) * 128],
                               start=True, stop=True)
            return ins
        E("PE", smm, reads=["Kts", "Xb"], writes=PM0)
        E("DVE", lambda h: h.tensor_mul(out=t["S0"][:, :], in0=t["S0"][:, :],
                                        in1=self.bc8("gCb")),
          reads=["S0", "gCb"], writes=["S0"])
        E("DVE", lambda h: h.tensor_add(out=t["S0"][:, :], in0=t["S0"][:, :],
                                        in1=p["pm0"][:, :]),
          reads=["S0"] + PM0, writes=["S0"])
        E("ACT", lambda h: h.activation(out=t["S0b"][:, :], in_=t["S0"][:, :],
                                        func=AF.Copy),
          reads=["S0"], writes=["S0b"])

    def _post_stage(self, seq, sub):
        t, d, p, E = self.t, self.d, self.p, self.E
        self.enter_phase("post")
        for half in range(2):
            l0 = half * 256
            oww, _, owr = self._win(full=True)
            self.dma("SPD", oww[:, :], d["s_ow"][:, :], writes=owr)
            self.bcast_row(t["r_mu"][0:1, l0:l0 + 256], "pb1", width=256,
                           reads=["r_mu"])
            self.bcast_row(t["r_inv"][0:1, l0:l0 + 256], "pb2", width=256,
                           reads=["r_inv"])
            for dc in range(8):
                if dc % 2 == 0:
                    pb, pbr = p["pm0"], ["pm0:0", "pm0:1"]
                    f0, f0r = t["fscr0"][:, 0:256], "fscr0"
                    f1, f1r = t["fscr1"][:, 0:256], "fscr1"
                else:
                    pb, pbr = p["pm1"], ["pm1:0", "pm1:1"]
                    f0, f0r = t["Esl"][:, 0:256], "Esl"
                    f1, f1r = t["Esl"][:, 256:512], "Esl"
                def grpo(h, oww=oww, dc=dc, l0=l0, pb=pb):
                    for di in range(8):
                        ins = h.matmul(
                            pb[:, 0:256],
                            oww[:, di * 1024 + dc * 128:di * 1024 + (dc + 1) * 128],
                            t["o2f"][:, di * 512 + l0:di * 512 + l0 + 256],
                            start=(di == 0), stop=(di == 7))
                    return ins
                E("PE", grpo, reads=owr + ["o2f"], writes=pbr)
                E("ACT", lambda h, pb=pb, f0=f0: h.activation(
                    out=f0, in_=pb[:, 0:256], func=AF.Copy, scale=0.5),
                  reads=pbr, writes=[f0r])
                xs = t["x_sub"][:, dc * 512 + l0:dc * 512 + l0 + 256]
                E("DVE", lambda h, xs=xs, f1=f1: h.tensor_sub(
                    out=f1, in0=xs, in1=p["pb1"][:, 0:256]),
                  reads=["x_sub", "pb1"], writes=[f1r])
                E("DVE", lambda h, f1=f1: h.tensor_mul(
                    out=f1, in0=f1, in1=p["pb2"][:, 0:256]),
                  reads=[f1r, "pb2"], writes=[f1r])
                E("DVE", lambda h, dc=dc, f0=f0, f1=f1: h.tensor_add(
                    out=t["a1sl"][:, dc * 256:(dc + 1) * 256],
                    in0=f1, in1=f0),
                  reads=[f1r, f0r], writes=["a1sl"])
                E("DVE", lambda h, xs=xs, dc=dc: h.tensor_add(
                    out=xs, in0=xs, in1=t["a1sl"][:, dc * 256:(dc + 1) * 256]),
                  reads=["x_sub", "a1sl"], writes=["x_sub"])
            E("ACT", lambda h: h.activation(out=t["xbfs"][:, 0:2048],
                                            in_=t["a1sl"][:, :], func=AF.Copy),
              reads=["a1sl"], writes=["xbfs"])
            E("ACT", lambda h: h.activation(out=t["xbfs"][:, 0:2048],
                                            in_=t["xbfs"][:, 0:2048], func=AF.Square),
              reads=["xbfs"], writes=["xbfs"])
            def h2sum(h):
                for dc in range(8):
                    ins = h.matmul(p["pb0"][0:1, 0:256], t["ones_colb"][:, :],
                                   t["xbfs"][:, dc * 256:(dc + 1) * 256],
                                   start=(dc == 0), stop=(dc == 7))
                return ins
            E("PE", h2sum, reads=["xbfs", "ones_colb"], writes=["pb0"])
            E("DVE", lambda h: h.tensor_scalar(
                out=t["r_t1"][0:1, 0:256], in0=p["pb0"][0:1, 0:256],
                scalar1=float(1.0 / 1024), scalar2=float(1e-6),
                op0=AL.mult, op1=AL.add), reads=["pb0"], writes=["r_t1"])
            E("ACT", lambda h: h.activation(out=t["r_t2"][0:1, 0:256],
                                            in_=t["r_t1"][0:1, 0:256], func=AF.Ln),
              reads=["r_t1"], writes=["r_t2"])
            E("ACT", lambda h: h.activation(out=t["r_t2"][0:1, 0:256],
                                            in_=t["r_t2"][0:1, 0:256],
                                            func=AF.Exp, scale=-0.5),
              reads=["r_t2"], writes=["r_t2"])
            self.bcast_row(t["r_t2"][0:1, 0:256], "pb0", width=256,
                           reads=["r_t2"])
            for dc in range(8):
                if dc % 2 == 0:
                    f1, f1r = t["fscr1"][:, 0:256], "fscr1"
                else:
                    f1, f1r = t["Esl"][:, 512:768], "Esl"
                E("DVE", lambda h, dc=dc, f1=f1: h.tensor_mul(
                    out=f1,
                    in0=t["a1sl"][:, dc * 256:(dc + 1) * 256],
                    in1=p["pb0"][:, 0:256]), reads=["a1sl", "pb0"], writes=[f1r])
                E("ACT", lambda h, dc=dc, f1=f1: h.activation(
                    out=t["h2sl"][:, dc * 256:(dc + 1) * 256],
                    in_=f1, func=AF.Identity,
                    scale=t["ln2c"][:, dc:dc + 1]),
                  reads=[f1r, "ln2c"], writes=["h2sl"])
            for hq in range(4):
                win, off, wr = self._win(full=True)
                in_ap = bass.AP(d["s_fcw"].tensor, hq * 1024,
                                [[32768, 128], [4096, 8], [1, 1024]])
                self.dma("SPD", win[:, :].rearrange("p (a j) -> p a j", j=1024),
                         in_ap, writes=wr)
                for hcl in range(8):
                    hc = hq * 8 + hcl
                    pb2_ = p[f"pb{1 + hc % 2}"]
                    pbr2 = f"pb{1 + hc % 2}"
                    def grpf(h, win=win, hcl=hcl, pb2_=pb2_):
                        for dc in range(8):
                            ins = h.matmul(
                                pb2_[:, 0:256],
                                win[:, dc * 1024 + hcl * 128:dc * 1024 + (hcl + 1) * 128],
                                t["h2sl"][:, dc * 256:(dc + 1) * 256],
                                start=(dc == 0), stop=(dc == 7))
                        return ins
                    E("PE", grpf, reads=wr + ["h2sl"], writes=[pbr2])
                    E("ACT", lambda h, pb2_=pb2_, hc=hc: h.activation(
                        out=t["fcsl"][:, hc * 256:(hc + 1) * 256],
                        in_=pb2_[:, 0:256], func=AF.Silu),
                      reads=[pbr2], writes=["fcsl"])
            for dc in range(8):
                win, off, wr = self._win(full=False)
                self.dma("SPD", win[:, off:off + 4096],
                         d["s_pw"][:, dc * 4096:(dc + 1) * 4096], writes=wr)
                pb2_ = p[f"pb{1 + dc % 2}"]
                pbr2 = f"pb{1 + dc % 2}"
                def grpp(h, win=win, off=off, pb2_=pb2_):
                    for hc in range(32):
                        ins = h.matmul(pb2_[:, 0:256],
                                       win[:, off + hc * 128:off + (hc + 1) * 128],
                                       t["fcsl"][:, hc * 256:(hc + 1) * 256],
                                       start=(hc == 0), stop=(hc == 31))
                    return ins
                E("PE", grpp, reads=wr + ["fcsl"], writes=[pbr2])
                if dc % 2 == 0:
                    f0, f0r = t["fscr0"][:, 0:256], "fscr0"
                else:
                    f0, f0r = t["Esl"][:, 768:1024], "Esl"
                E("ACT", lambda h, pb2_=pb2_, f0=f0: h.activation(
                    out=f0, in_=pb2_[:, 0:256], func=AF.Copy,
                    scale=0.5), reads=[pbr2], writes=[f0r])
                xs = t["x_sub"][:, dc * 512 + l0:dc * 512 + l0 + 256]
                E("DVE", lambda h, xs=xs, f0=f0: h.tensor_add(
                    out=xs, in0=xs, in1=f0),
                  reads=["x_sub", f0r], writes=["x_sub"])

    def build(self):
        import os
        stages = os.environ.get("ANT_STAGES", "full")
        self.prep()
        for seq in range(SEQ_PER_CORE):
            self.load_x_tile(seq)
            if stages in ("ffn1", "ffn1attn", "full"):
                self.ffn_half(seq)
            if stages in ("ffn1attn", "full"):
                self.attn(seq)
            if stages == "full":
                self.ffn_half(seq)
            self.store_x_tile(seq)
        self.em.final_wait_all("POOL")


def build_program():
    _register_ops()
    nc = bass.Bass("TRN2", target_bir_lowering=False)
    es = ExitStack()
    kb = KB(nc, es)
    with es, nc.Block() as block:
        kb.build()
        kb.em.install(block)
    lower_extended_insts(nc)
    return nc, kb



_PROG_CACHE = {}


def kernel(L0_attn_x, L0_attn_ln1_w, L0_attn_qw, L0_attn_kw, L0_attn_vw,
           L0_attn_bw, L0_attn_aw, L0_attn_A_log, L0_attn_dt_bias, L0_attn_gw,
           L0_attn_onorm_w, L0_attn_ow, L0_attn_ln2_w, L0_attn_fcw, L0_attn_pw,
           L0_ffn_w1, L0_ffn_b1, L0_ffn_w2, L0_ffn_b2):
    consts = _get_host_consts()
    if "nc" not in _PROG_CACHE:
        nc, kb = build_program()
        _PROG_CACHE["nc"] = nc
    nc = _PROG_CACHE["nc"]

    f32 = np.float32
    asf = lambda a: np.ascontiguousarray(a, dtype=f32)
    shared = {
        "qw": asf(L0_attn_qw), "kw": asf(L0_attn_kw), "vw": asf(L0_attn_vw),
        "gw": asf(L0_attn_gw), "ow": asf(L0_attn_ow), "fcw": asf(L0_attn_fcw),
        "pw": asf(L0_attn_pw), "w1": asf(L0_ffn_w1), "w2": asf(L0_ffn_w2),
        "awbw": asf(np.concatenate([np.asarray(L0_attn_aw),
                                    np.asarray(L0_attn_bw)], axis=1)),
        "b1r": asf(np.asarray(L0_ffn_b1).reshape(32, 128).T),
        "b2r": asf(np.asarray(L0_ffn_b2).reshape(8, 128).T),
        "ln1c": asf(np.asarray(L0_attn_ln1_w).reshape(8, 128).T),
        "ln2c": asf(np.asarray(L0_attn_ln2_w).reshape(8, 128).T),
        "onormc": asf(np.asarray(L0_attn_onorm_w).reshape(128, 1)),
        "alog": asf(np.asarray(L0_attn_A_log).reshape(8, 1)),
        "dtb": asf(np.asarray(L0_attn_dt_bias).reshape(8, 1)),
        "identb": consts["identb"], "identf": consts["identf"],
        "sel8": consts["sel8"], "e8": consts["e8"],
        "trineg": consts["trineg"], "trinc": consts["trinc"],
        "identr": consts["identr"],
    }
    x = asf(L0_attn_x)
    in_maps = []
    for core in range(NCORE):
        im = dict(shared)
        im["x"] = np.ascontiguousarray(x[core * 2:(core + 1) * 2])
        im["m32f"] = np.ascontiguousarray(
            _feature_major_mask(consts["m32"][core * 2:(core + 1) * 2])
        ).astype(ml_dtypes.bfloat16)
        im["mdf"] = np.ascontiguousarray(
            _feature_major_mask(consts["md"][core * 2:(core + 1) * 2])
        ).astype(ml_dtypes.bfloat16)
        in_maps.append(im)
    res = run_bass_kernel_spmd(nc, in_maps, core_ids=list(range(NCORE)))
    out = np.concatenate([np.asarray(r["y"], dtype=np.float32)
                          for r in res.results], axis=0)
    return out.reshape(16, T, D)
